# version 07
# baseline (speedup 1.0000x reference)
import math
import numpy as np
import ml_dtypes
from contextlib import ExitStack

import concourse.bass as bass
import concourse.mybir as mybir
from concourse.bass_utils import run_bass_kernel_spmd

F32 = mybir.dt.float32
BF16 = mybir.dt.bfloat16
AF = mybir.ActivationFunctionType
ALU = mybir.AluOpType
AX = mybir.AxisListType

B, T, C, H, D, FF = 4, 4096, 2048, 16, 128, 5632
NT = 32
OWN = 16
NCC = C // 128
NHC = FF // 128
SCALE = 1.0 / math.sqrt(D)
LN_EPS = 1e-5
RMS_EPS = 1.1920929e-07
QEPS = 1e-8

ENGINES = ("sync", "scalar", "tensor", "vector", "gpsimd")


class Pl:

    def __init__(self, sems):
        self.q = {e: [] for e in ENGINES}
        self.sems = sems
        self.cnt = {k: 0 for k in sems}
        self.last_wait = {e: {} for e in ENGINES}

    def op(self, eng, fn, waits=(), inc=None, amt=1):
        eff = []
        for wv in waits:
            if wv is None:
                continue
            name, val = wv
            if val is None or val <= 0:
                continue
            if self.last_wait[eng].get(name, -1) >= val:
                continue
            self.last_wait[eng][name] = val
            eff.append((self.sems[name], val))
        if inc is not None:
            self.cnt[inc] += amt
        h = self.sems[inc] if inc is not None else None

        def g(e, _w=tuple(eff), _fn=fn, _h=h, _amt=amt):
            for (s, v) in _w:
                e.wait_ge(s, v)
            r = _fn(e)
            if _h is not None:
                r.then_inc(_h, _amt)
        self.q[eng].append(g)
        return self.cnt[inc] if inc is not None else None


def build_program(upto="D"):
    nc = bass.Bass("TRN2", target_bir_lowering=False)

    x = nc.dram_tensor("x", [T, C], F32, kind="ExternalInput").ap()
    wqkv = nc.dram_tensor("wqkv", [3 * C, C], F32, kind="ExternalInput").ap()
    wo = nc.dram_tensor("wo", [C, C], F32, kind="ExternalInput").ap()
    w1 = nc.dram_tensor("w1", [FF, C], F32, kind="ExternalInput").ap()
    w2 = nc.dram_tensor("w2", [FF, C], F32, kind="ExternalInput").ap()
    w3 = nc.dram_tensor("w3", [C, FF], F32, kind="ExternalInput").ap()
    sincos = nc.dram_tensor("sincos", [2, T, D], F32, kind="ExternalInput").ap()
    masks = nc.dram_tensor("masks", [8, 128, 512], BF16, kind="ExternalInput").ap()
    ident = nc.dram_tensor("ident", [128, 128], BF16, kind="ExternalInput").ap()
    out = nc.dram_tensor("out", [OWN * 128, C], F32, kind="ExternalOutput").ap()

    wqkv_bf = nc.dram_tensor("wqkv_bf", [3 * C, C], BF16, kind="Internal").ap()
    wo_bf = nc.dram_tensor("wo_bf", [C, C], BF16, kind="Internal").ap()
    qkvscr = nc.dram_tensor("qkvscr", [12, NT, 128, 512], BF16, kind="Internal").ap()
    w1qt = nc.dram_tensor("w1qt", [NHC, 128, 2048], BF16, kind="Internal").ap()
    w2qt = nc.dram_tensor("w2qt", [NHC, 128, 2048], BF16, kind="Internal").ap()
    w3q = nc.dram_tensor("w3q", [C, FF], BF16, kind="Internal").ap()
    sscr = nc.dram_tensor("sscr", [OWN, 128, 2048], BF16, kind="Internal").ap()
    w3qt = nc.dram_tensor("w3qt", [NHC, 128, 2048], BF16, kind="Internal").ap()
    woqt = nc.dram_tensor("woqt", [NCC, 128, 2048], BF16, kind="Internal").ap()

    es = ExitStack()
    E = es.enter_context
    xnT_own = E(nc.sbuf_tensor("xnT_own", [128, OWN * 2048], BF16))
    pool16 = E(nc.sbuf_tensor("pool16", [128, 16 * 2048], BF16))
    util = E(nc.sbuf_tensor("util", [128, 24576], BF16))
    masks_sb = E(nc.sbuf_tensor("masks_sb", [128, 4096], BF16))
    scs = E(nc.sbuf_tensor("scs", [128, 512], F32))
    attnT = E(nc.sbuf_tensor("attnT", [128, 1024], BF16))
    amsk = E(nc.sbuf_tensor("amsk", [128, 1024], BF16))
    avf = E(nc.sbuf_tensor("avf", [128, 512], F32))
    rden = E(nc.sbuf_tensor("rden", [1, 512], F32))
    ones1 = E(nc.sbuf_tensor("ones1", [1, 128], F32))
    ones128 = E(nc.sbuf_tensor("ones128", [128, 1], F32))
    onesb = E(nc.sbuf_tensor("onesb", [128, 1], BF16))
    ident_sb = E(nc.sbuf_tensor("ident_sb", [128, 128], BF16))
    lnv = E(nc.sbuf_tensor("lnv", [128, 16], F32))
    rmsv = E(nc.sbuf_tensor("rmsv", [128, 24], F32))
    statc = E(nc.sbuf_tensor("statc", [128, 128], F32))
    stat1 = E(nc.sbuf_tensor("stat1", [128, 2], F32))
    tau2 = E(nc.sbuf_tensor("tau2", [1, 2], F32))
    thrw = E(nc.sbuf_tensor("thrw", [128, 6], F32))
    sigf = E(nc.sbuf_tensor("sigf", [128, 1024], F32))
    gstg = E(nc.sbuf_tensor("gstg", [128, 512], F32))
    sgst = E(nc.sbuf_tensor("sgst", [128, 512], F32))
    ostg = E(nc.sbuf_tensor("ostg", [128, 1024], F32))
    epsv = E(nc.sbuf_tensor("epsv", [128, 4], F32))

    utilf = util.ap().bitcast(F32)

    def U(c0, w):
        return util[:, c0:c0 + w]

    def UF(c0, w):
        return utilf[:, c0:c0 + w]

    P = [E(nc.psum_tensor("P%d" % i, [128, 512], F32)) for i in range(8)]
    Pb = [p.ap().bitcast(BF16) for p in P]

    sems = {}
    for name in ("sLD", "sSC", "sGP", "sOUT", "sPE", "sACT", "sDVE", "sSQ"):
        sems[name] = E(nc.semaphore(name))
    pl = Pl(sems)

    def dma(eng, out_ap, in_ap, waits=(), inc=None, transpose=False):
        def f(e, o=out_ap, i=in_ap, t=transpose):
            return e.dma_start(out=o, in_=i, transpose=True) if t \
                else e.dma_start(out=o, in_=i)
        return pl.op(eng, f, waits=waits, inc=inc, amt=16)

    def t_mm(out_ap, lhsT, rhs, start, stop, waits=(), inc=None):
        def f(e, o=out_ap, lt=lhsT, r=rhs, s1=start, s2=stop):
            return e.matmul(o, lt, r, start=s1, stop=s2)
        return pl.op("tensor", f, waits=waits, inc=inc)

    def t_tr(out_ap, in_ap, waits=(), inc=None):
        def f(e, o=out_ap, i=in_ap):
            return e.transpose(o, i, ident_sb[:])
        return pl.op("tensor", f, waits=waits, inc=inc)

    def a_act(out_ap, in_ap, func, scale=1.0, bias=0.0, accum=None, waits=()):
        def f(e, o=out_ap, i=in_ap, fu=func, s=scale, b=bias, a=accum):
            kw = {}
            if a is not None:
                kw["accum_out"] = a
            return e.activation(out=o, in_=i, func=fu, scale=s, bias=b, **kw)
        return pl.op("scalar", f, waits=waits, inc="sACT")

    def a_copy(out_ap, in_ap, waits=()):
        def f(e, o=out_ap, i=in_ap):
            return e.activation(out=o, in_=i, func=AF.Copy)
        return pl.op("scalar", f, waits=waits, inc="sACT")

    def v_red(out_ap, in_ap, op=ALU.add, absv=None, waits=()):
        def f(e, o=out_ap, i=in_ap, _op=op, _a=absv):
            return e.tensor_reduce(out=o, in_=i, axis=AX.X, op=_op,
                                   apply_absolute_value=_a)
        return pl.op("vector", f, waits=waits, inc="sDVE")

    def v_tt(out_ap, a_ap, b_ap, op=ALU.mult, waits=()):
        def f(e, o=out_ap, a=a_ap, b=b_ap, _op=op):
            return e.tensor_tensor(out=o, in0=a, in1=b, op=_op)
        return pl.op("vector", f, waits=waits, inc="sDVE")

    def v_stt(out_ap, in0, scalar, in1, op0, op1, accum=None, waits=()):
        def f(e, o=out_ap, a=in0, s=scalar, b=in1, o0=op0, o1=op1, ac=accum):
            kw = {}
            if ac is not None:
                kw["accum_out"] = ac
            return e.scalar_tensor_tensor(out=o, in0=a, scalar=s, in1=b,
                                          op0=o0, op1=o1, **kw)
        return pl.op("vector", f, waits=waits, inc="sDVE")

    def v_ts(out_ap, in_ap, s1, op0, waits=()):
        def f(e, o=out_ap, i=in_ap, _s=s1, _o0=op0):
            return e.tensor_scalar(out=o, in0=i, scalar1=_s, scalar2=None,
                                   op0=_o0)
        return pl.op("vector", f, waits=waits, inc="sDVE")

    def v_tsmul(out_ap, in_ap, c, waits=()):
        def f(e, o=out_ap, i=in_ap, _c=c):
            return e.tensor_scalar_mul(o, i, _c)
        return pl.op("vector", f, waits=waits, inc="sDVE")

    def v_copy(out_ap, in_ap, waits=()):
        def f(e, o=out_ap, i=in_ap):
            return e.tensor_copy(o, i)
        return pl.op("vector", f, waits=waits, inc="sDVE")

    def v_recip(out_ap, in_ap, waits=()):
        def f(e, o=out_ap, i=in_ap):
            return e.reciprocal(o, i)
        return pl.op("vector", f, waits=waits, inc="sDVE")

    def v_memset(ap_, val, waits=()):
        def f(e, o=ap_, v=val):
            return e.memset(o, v)
        return pl.op("vector", f, waits=waits, inc="sDVE")

    def nopwait(eng, waits):
        def f(e):
            return e.nop()
        return pl.op(eng, f, waits=waits)

    def own_xnT(cc, col, w=128):
        return xnT_own[:, cc * 2048 + col: cc * 2048 + col + w]

    def oth_xnT(cc, col, w=128):
        return pool16[:, cc * 2048 + col: cc * 2048 + col + w]

    def xnT_tile(cc, l):
        return own_xnT(cc, l * 128) if l < OWN else oth_xnT(cc, (l - OWN) * 128)

    def avT(h, col, w):
        return pool16[:, h * 2048 + col: h * 2048 + col + w]

    def hid(hc, col, w):
        return pool16[:, hc * 512 + col: hc * 512 + col + w]

    for mi in range(8):
        dma("sync", masks_sb[:, mi * 512:(mi + 1) * 512], masks[mi],
            inc="sLD")
    dma("sync", ident_sb[:], ident, inc="sLD")
    consts_ld = pl.cnt["sLD"]
    v_memset(ones1[:], 1.0)
    v_memset(ones128[:], 1.0)
    v_memset(onesb[:], 1.0)
    v_memset(epsv[:, 0:1], LN_EPS)
    v_memset(epsv[:, 1:2], RMS_EPS)
    v_memset(epsv[:, 2:3], 0.5 * QEPS)
    consts_dve = pl.cnt["sDVE"]

    qkv_cast = []
    for i in range(6):
        qkv_cast.append(dma("gpsimd", wqkv_bf[i * 1024:(i + 1) * 1024, :],
                            wqkv[i * 1024:(i + 1) * 1024, :], inc="sGP"))
    wo_cast = []
    for i in range(2):
        wo_cast.append(dma("gpsimd", wo_bf[i * 1024:(i + 1) * 1024, :],
                           wo[i * 1024:(i + 1) * 1024, :], inc="sGP"))

    wflat = {
        0: w1.flatten().rearrange("(r p c) -> r p c", p=128, c=704),
        1: w2.flatten().rearrange("(r p c) -> r p c", p=128, c=704),
        2: w3.flatten().rearrange("(r p c) -> r p c", p=128, c=704),
    }
    NW = float(FF * C)
    sq_state = {
        "sbuf_free": [None, None], "sld": {}, "thr": [None, None, None],
        "ta_prev": None, "cp3_prev": None,
        "qwf_free": [None, None], "qld": {}, "qt_free": None,
        "st_free": [None, None], "cp_prev": None, "first_tr": True,
        "first_q": [True, True], "ii": 0, "ic": 0,
    }
    sq_jobs = []

    def stat_issue(wi, r, jid):
        def go():
            par = jid % 2
            wf = UF(10752 + par * 704, 704)
            wv = []
            if sq_state["sbuf_free"][par] is not None:
                wv.append(("sDVE", sq_state["sbuf_free"][par]))
            sq_state["sld"][jid] = dma("sync", wf, wflat[wi][r], waits=wv,
                                       inc="sSQ")
        return go

    def stat_comp(wi, r, jid):
        def go():
            par = jid % 2
            wf = UF(10752 + par * 704, 704)
            dv = v_red(statc[:, r:r + 1], wf, absv=True,
                       waits=[("sSQ", pl.cnt["sSQ"])])
            sq_state["sbuf_free"][par] = dv
        return go

    def statfin_comp(wi):
        def go():
            dv = v_red(stat1[:, 0:1], statc[:, 0:128],
                       waits=[("sDVE", sq_state["sbuf_free"][0]),
                              ("sDVE", sq_state["sbuf_free"][1])])
            wv = [("sDVE", dv), ("sDVE", consts_dve)]
            if sq_state["ta_prev"] is not None:
                wv.append(("sACT", sq_state["ta_prev"]))
            pe_v = t_mm(P[2][0:1, 0:1], ones128[:], stat1[:, 0:1], True, True,
                        waits=wv, inc="sPE")
            ta = a_act(tau2[0:1, 0:1], P[2][0:1, 0:1], AF.Identity,
                       scale=0.5 / NW, bias=epsv[0:1, 2:3],
                       waits=[("sPE", pe_v), ("sDVE", consts_dve)])
            sq_state["ta_prev"] = ta
            nt_ = v_tsmul(tau2[0:1, 1:2], tau2[0:1, 0:1], -1.0,
                          waits=[("sACT", ta)])
            wv = [("sDVE", nt_)]
            if sq_state["cp3_prev"] is not None:
                wv.append(("sDVE", sq_state["cp3_prev"]))
            pe_v = t_mm(P[3][:, 0:2], ones1[:], tau2[0:1, 0:2], True, True,
                        waits=wv, inc="sPE")
            cp3 = v_copy(thrw[:, wi * 2:wi * 2 + 2], P[3][:, 0:2],
                         waits=[("sPE", pe_v)])
            sq_state["cp3_prev"] = cp3
            sq_state["thr"][wi] = cp3
        return go

    def q12_issue(wi, hc, half, jid):
        wsrc = w1 if wi == 0 else w2
        def go():
            par = jid % 2
            wf = UF(4096 + par * 1024, 1024)
            wv = []
            if sq_state["first_q"][par]:
                wv += a0_done
                sq_state["first_q"][par] = False
            if sq_state["qwf_free"][par] is not None:
                wv.append(("sDVE", sq_state["qwf_free"][par]))
            sq_state["qld"][jid] = dma(
                "sync", wf, wsrc[hc * 128:(hc + 1) * 128,
                              half * 1024:(half + 1) * 1024],
                waits=wv, inc="sSQ")
        return go

    def q12_comp(wi, hc, half, jid):
        wdst = w1qt if wi == 0 else w2qt
        def go():
            par = jid % 2
            wf = UF(4096 + par * 1024, 1024)
            qt = U(12288, 1024)
            lt = U(13312, 1024)
            st = U(14336 + (jid % 2) * 1024, 1024)
            wv = [("sSQ", pl.cnt["sSQ"]),
                  ("sDVE", sq_state["thr"][wi])]
            if sq_state["qt_free"] is not None:
                wv.append(sq_state["qt_free"])
            v_ts(qt, wf, thrw[:, wi * 2:wi * 2 + 1], ALU.is_gt, waits=wv)
            l2 = v_ts(lt, wf, thrw[:, wi * 2 + 1:wi * 2 + 2], ALU.is_lt)
            qd = v_tt(qt, qt, lt, ALU.subtract, waits=[("sDVE", l2)])
            sq_state["qwf_free"][par] = qd
            cp = sq_state["cp_prev"]
            for g4 in range(2):
                bank = 6 + (g4 % 2)
                wvt = [("sDVE", qd)]
                if cp is not None:
                    wvt.append(("sACT", cp))
                elif sq_state["first_tr"]:
                    wvt += a0_done
                    sq_state["first_tr"] = False
                pe_v = None
                for k in range(4):
                    dc = g4 * 4 + k
                    pe_v = t_tr(Pb[bank][:, k * 128:(k + 1) * 128],
                                qt[:, dc * 128:(dc + 1) * 128],
                                waits=wvt if k == 0 else (),
                                inc="sPE" if k == 3 else None)
                wvc = [("sPE", pe_v)]
                if g4 == 0 and sq_state["st_free"][par] is not None:
                    wvc.append(("sGP", sq_state["st_free"][par]))
                cp = a_copy(st[:, g4 * 512:(g4 + 1) * 512],
                            Pb[bank][:, 0:512], waits=wvc)
            sq_state["qt_free"] = ("sACT", cp)
            sq_state["cp_prev"] = cp
            sq_state["st_free"][par] = dma(
                "gpsimd", wdst[hc][:, half * 1024:(half + 1) * 1024], st,
                waits=[("sACT", cp)], inc="sGP")
        return go

    def q3_issue(r, ci, cw, jid):
        def go():
            par = jid % 2
            wf = UF(4096 + par * 1024, cw)
            wv = []
            if sq_state["first_q"][par]:
                wv += a0_done
                sq_state["first_q"][par] = False
            if sq_state["qwf_free"][par] is not None:
                wv.append(("sDVE", sq_state["qwf_free"][par]))
            sq_state["qld"][jid] = dma(
                "sync", wf, w3[r * 128:(r + 1) * 128, ci:ci + cw],
                waits=wv, inc="sSQ")
        return go

    def q3_comp(r, ci, cw, jid):
        def go():
            par = jid % 2
            wf = UF(4096 + par * 1024, cw)
            qt = U(12288, cw)
            lt = U(13312, cw)
            wv = [("sSQ", pl.cnt["sSQ"]),
                  ("sDVE", sq_state["thr"][2])]
            if sq_state["qt_free"] is not None:
                wv.append(sq_state["qt_free"])
            v_ts(qt, wf, thrw[:, 4:5], ALU.is_gt, waits=wv)
            l2 = v_ts(lt, wf, thrw[:, 5:6], ALU.is_lt)
            qd = v_tt(qt, qt, lt, ALU.subtract, waits=[("sDVE", l2)])
            sq_state["qwf_free"][par] = qd
            gp = dma("gpsimd", w3q[r * 128:(r + 1) * 128, ci:ci + cw], qt,
                     waits=[("sDVE", qd)], inc="sGP")
            sq_state["qt_free"] = ("sGP", gp)
        return go

    def noop():
        pass

    jid = 0
    for wi in range(3):
        for r in range(128):
            sq_jobs.append((stat_issue(wi, r, jid), stat_comp(wi, r, jid)))
            jid += 1
        sq_jobs.append((noop, statfin_comp(wi)))
        jid += 1
    qjid = 0
    for wi in range(2):
        for hc in range(NHC):
            for half in range(2):
                sq_jobs.append((q12_issue(wi, hc, half, qjid),
                                q12_comp(wi, hc, half, qjid)))
                qjid += 1
    for r in range(16):
        for ci in (0, 1024, 2048, 3072, 4096, 5120):
            cw = 1024 if ci < 5120 else 512
            sq_jobs.append((q3_issue(r, ci, cw, qjid),
                            q3_comp(r, ci, cw, qjid)))
            qjid += 1
    N_STAT_JOBS = 3 * 129

    def sq_pump(k, allow_quant):
        for _ in range(k):
            lim = len(sq_jobs) if allow_quant else N_STAT_JOBS
            if sq_state["ii"] < lim and sq_state["ii"] - sq_state["ic"] < 2:
                sq_jobs[sq_state["ii"]][0]()
                sq_state["ii"] += 1
            if sq_state["ic"] < sq_state["ii"] and \
                    (sq_state["ii"] - sq_state["ic"] >= 2 or
                     sq_state["ii"] >= lim):
                sq_jobs[sq_state["ic"]][1]()
                sq_state["ic"] += 1

    a0_apply = [None] * NT
    a0_var = [None] * NT
    a0_evac = [[None] * NCC for _ in range(NT)]
    for l in range(NT):
        sq_pump(3, allow_quant=False)
        par = l % 2
        xb = UF(0 + par * 2048, 2048)
        sq = UF(4096, 2048)
        xnbf = U(12288 + par * 2048, 2048)
        lv = lnv[:, par * 8:par * 8 + 8]
        w = []
        if l >= 2:
            w.append(("sACT", a0_apply[l - 2]))
        ldx = dma("sync", xb, x[l * 128:(l + 1) * 128, :], waits=w, inc="sLD")
        w = [("sLD", ldx)]
        if l >= 2:
            w.append(("sACT", a0_apply[l - 2]))
        r1 = v_red(lv[:, 0:1], xb, waits=w)
        w = [("sLD", ldx)]
        if l >= 2:
            w.append(("sDVE", a0_var[l - 2]))
        sq_a = a_act(sq, xb, AF.Square, accum=lv[:, 1:2], waits=w)
        r2 = v_tsmul(lv[:, 2:3], lv[:, 0:1], 1.0 / C, waits=[("sDVE", r1)])
        r3 = v_tt(lv[:, 3:4], lv[:, 2:3], lv[:, 2:3], ALU.mult,
                  waits=[("sDVE", r2)])
        var = v_stt(lv[:, 4:5], lv[:, 1:2], 1.0 / C, lv[:, 3:4],
                    ALU.mult, ALU.subtract,
                    waits=[("sACT", sq_a), ("sDVE", r3)])
        a0_var[l] = var
        sqv = a_act(lv[:, 5:6], lv[:, 4:5], AF.Sqrt, bias=epsv[:, 0:1],
                    waits=[("sDVE", var), ("sDVE", consts_dve)])
        rc0 = v_recip(lv[:, 6:7], lv[:, 5:6], waits=[("sACT", sqv)])
        nmr = v_stt(lv[:, 7:8], lv[:, 2:3], -1.0, lv[:, 6:7], ALU.mult,
                    ALU.mult, waits=[("sDVE", rc0)])
        w = [("sDVE", nmr)]
        if l >= 2:
            w.append(a0_evac[l - 2][NCC - 1])
        ap_a = a_act(xnbf, xb, AF.Identity, scale=lv[:, 6:7], bias=lv[:, 7:8],
                     waits=w)
        a0_apply[l] = ap_a
        dcol = l * 128 if l < OWN else (l - OWN) * 128
        dest = own_xnT if l < OWN else oth_xnT
        for cc in range(NCC):
            bank = 6 + (cc % 2)
            w = [("sACT", ap_a), ("sLD", consts_ld)]
            if cc >= 2:
                w.append(a0_evac[l][cc - 2])
            elif l >= 1:
                w.append(a0_evac[l - 1][NCC - 2 + cc])
            pe_v = t_tr(Pb[bank][:, 0:128], xnbf[:, cc * 128:(cc + 1) * 128],
                        waits=w, inc="sPE")
            if cc % 2 == 0:
                cnt = v_copy(dest(cc, dcol), Pb[bank][:, 0:128],
                             waits=[("sPE", pe_v)])
                a0_evac[l][cc] = ("sDVE", cnt)
            else:
                cnt = a_copy(dest(cc, dcol), Pb[bank][:, 0:128],
                             waits=[("sPE", pe_v)])
                a0_evac[l][cc] = ("sACT", cnt)
    a0_done = [("sACT", pl.cnt["sACT"]), ("sDVE", pl.cnt["sDVE"])]

    if upto == "A0":
        wv = list(a0_done)
        for cc in range(16):
            dma("gpsimd", out[cc * 128:(cc + 1) * 128, 0:2048],
                own_xnT(cc, 0, 2048), waits=wv, inc="sOUT")
        nopwait("sync", [("sOUT", pl.cnt["sOUT"])])
        return nc, es, pl

    wT = U(0, 8192)

    def wT_tile(cc):
        return wT[:, cc * 512:(cc + 1) * 512]

    q2 = UF(8192, 512)
    tmp2 = UF(8704, 512)
    sqs = UF(9216, 512)
    q3 = UF(9728, 512)
    a1_mm_done = None
    a1_bank_free = [None, None]
    a1_stage_free = [None, None]
    a1_scs_free = [None, None]
    a1_q2_free = None
    for osl in range(12):
        wv = [("sGP", qkv_cast[osl // 2])] + a0_done
        if a1_mm_done is not None:
            wv.append(("sPE", a1_mm_done))
        wld = None
        for cc in range(NCC):
            src = wqkv_bf[osl * 512:(osl + 1) * 512, cc * 128:(cc + 1) * 128]
            wld = dma("scalar", wT_tile(cc), src, waits=wv if cc == 0 else (),
                      inc="sSC", transpose=True)
        lrange = range(OWN) if osl < 4 else range(NT)
        for l in lrange:
            sq_pump(2, allow_quant=True)
            par = l % 2
            bank = par
            is_v = osl >= 8
            scl = None
            if not is_v:
                wv = []
                if a1_scs_free[par] is not None:
                    wv.append(("sDVE", a1_scs_free[par]))
                dma("sync", scs[:, par * 256:par * 256 + 128],
                    sincos[0, l * 128:(l + 1) * 128, :], waits=wv, inc="sLD")
                scl = dma("sync", scs[:, par * 256 + 128:par * 256 + 256],
                          sincos[1, l * 128:(l + 1) * 128, :], inc="sLD")
            wv = [("sSC", wld)] + a0_done
            if a1_bank_free[par] is not None:
                wv.append(a1_bank_free[par])
            pe_v = None
            for cc in range(NCC):
                pe_v = t_mm(P[bank][:, 0:512], xnT_tile(cc, l), wT_tile(cc),
                            start=(cc == 0), stop=(cc == NCC - 1),
                            waits=wv if cc == 0 else (),
                            inc="sPE" if cc == NCC - 1 else None)
            a1_mm_done = pe_v
            st = U(20480 + par * 512, 512)
            if is_v:
                w2v = [("sPE", pe_v)]
                if a1_stage_free[par] is not None:
                    w2v.append(("sGP", a1_stage_free[par]))
                cp = a_copy(st, P[bank][:, 0:512], waits=w2v)
                a1_bank_free[par] = ("sACT", cp)
                a1_stage_free[par] = dma("gpsimd", qkvscr[osl, l], st,
                                         waits=[("sACT", cp)], inc="sGP")
            else:
                rv = rmsv[:, par * 12:par * 12 + 12]
                sinb = scs[:, par * 256:par * 256 + 64]
                cosb = scs[:, par * 256 + 128:par * 256 + 256]
                sin_b = sinb.unsqueeze(1).broadcast_to([128, 4, 64])
                cos_b = cosb.unsqueeze(1).broadcast_to([128, 4, 128])
                pk3 = P[bank][:, 0:512].rearrange("p (h z) -> p h z", h=4)
                q23 = q2.rearrange("p (h z) -> p h z", h=4)
                wv = [("sPE", pe_v), ("sLD", scl)]
                if a1_q2_free is not None:
                    wv.append(("sACT", a1_q2_free))
                v_stt(q23[:, :, 0:64], pk3[:, :, 64:128], -1.0, sin_b,
                      ALU.mult, ALU.mult, waits=wv)
                v_stt(q23[:, :, 64:128], pk3[:, :, 0:64], 1.0, sin_b,
                      ALU.mult, ALU.mult)
                cosm = v_tt(tmp2.rearrange("p (h z) -> p h z", h=4),
                            pk3, cos_b, ALU.mult)
                a1_bank_free[par] = ("sDVE", cosm)
                a1_scs_free[par] = cosm
                ad = v_tt(q3, q2, tmp2, ALU.add, waits=[("sDVE", cosm)])
                for hh in range(4):
                    v_stt(sqs[:, hh * 128:(hh + 1) * 128],
                          q3[:, hh * 128:(hh + 1) * 128], 1.0,
                          q3[:, hh * 128:(hh + 1) * 128], ALU.mult, ALU.mult,
                          accum=rv[:, hh:hh + 1],
                          waits=[("sDVE", ad)] if hh == 0 else ())
                dv = pl.cnt["sDVE"]
                sr = a_act(rv[:, 4:8], rv[:, 0:4], AF.Sqrt, scale=1.0 / D,
                           bias=epsv[:, 1:2],
                           waits=[("sDVE", dv), ("sDVE", consts_dve)])
                rc = v_recip(rv[:, 8:12], rv[:, 4:8], waits=[("sACT", sr)])
                w2v = [("sDVE", rc)]
                if a1_stage_free[par] is not None:
                    w2v.append(("sGP", a1_stage_free[par]))
                cp = None
                for hh in range(4):
                    cp = a_act(st[:, hh * 128:(hh + 1) * 128],
                               q3[:, hh * 128:(hh + 1) * 128], AF.Identity,
                               scale=rv[:, 8 + hh:9 + hh],
                               waits=w2v if hh == 0 else ())
                a1_q2_free = cp
                a1_stage_free[par] = dma("gpsimd", qkvscr[osl, l], st,
                                         waits=[("sACT", cp)], inc="sGP")
    a1_done_gp = pl.cnt["sGP"]
    a1_done = [("sACT", pl.cnt["sACT"]), ("sDVE", pl.cnt["sDVE"]),
               ("sPE", pl.cnt["sPE"])]

    if upto == "A1":
        wv = [("sGP", a1_done_gp)]
        for i, (osl, l) in enumerate([(0, 0), (0, 15), (5, 0), (5, 17),
                                      (9, 0), (9, 17), (3, 15), (7, 31)]):
            dma("gpsimd", out[i * 128:(i + 1) * 128, 0:512],
                qkvscr[osl, l], waits=wv, inc="sOUT")
        nopwait("sync", [("sOUT", pl.cnt["sOUT"])])
        return nc, es, pl

    while sq_state["ic"] < len(sq_jobs):
        sq_pump(1, allow_quant=True)
    thr_dve = pl.cnt["sDVE"]
    stats_act = pl.cnt["sACT"]
    quant_gp = pl.cnt["sGP"]
    quant_act = pl.cnt["sACT"]
    quant_dve = pl.cnt["sDVE"]
    quant_pe = pl.cnt["sPE"]

    if upto == "QUANT":
        wv = [("sGP", quant_gp)]
        dma("gpsimd", out[0:128, 0:2048], w1qt[3], waits=wv, inc="sOUT")
        dma("gpsimd", out[128:256, 0:2048], w2qt[40], waits=wv, inc="sOUT")
        dma("gpsimd", out[256:384, 0:2048], w3q[512:640, 1024:3072], waits=wv,
            inc="sOUT")
        nopwait("sync", [("sOUT", pl.cnt["sOUT"])])
        return nc, es, pl

    def b_bufs(hp):
        if hp == 0:
            return U(0, 4096), U(4096, 4096), U(8192, 2048)
        return U(10240, 4096), U(14336, 4096), U(18432, 2048)

    def b_issue_loads(h, wv):
        kTb, vbb, qTb = b_bufs(h % 2)
        os_q, os_k, os_v = h // 4, 4 + h // 4, 8 + h // 4
        dcol = (h % 4) * 128
        ksl = qkvscr[os_k][:, :, dcol:dcol + 128].rearrange("l t d -> (l t) d")
        kld = dma("scalar", kTb, ksl, waits=wv, inc="sSC", transpose=True)
        qsl = qkvscr[os_q][0:OWN][:, :, dcol:dcol + 128].rearrange(
            "l t d -> (l t) d")
        qld = dma("scalar", qTb, qsl, inc="sSC", transpose=True)
        vld = None
        for l in range(NT):
            vld = dma("sync", vbb[:, l * 128:(l + 1) * 128],
                      qkvscr[os_v, l][:, dcol:dcol + 128],
                      waits=wv if l == 0 else (), inc="sLD")
        return qld, vld

    b_prev_pe = [None, None]
    exp_done = [None, None]
    mask_done = [None, None]
    avden_done = [None, None]
    cp_prev = None
    nv_prev = None
    b_first = [("sACT", max(stats_act, quant_act)), ("sDVE", thr_dve),
               ("sGP", a1_done_gp), ("sGP", quant_gp), ("sDVE", quant_dve),
               ("sPE", quant_pe)]
    pt_jobs = []
    pt_state = {"free": [None, None], "done": [0]}

    def mk_pt(src_ap, dst_ap, idx):
        def go():
            par = idx % 2
            tb_ = U(20480 + par * 2048, 2048)
            wv = [("sGP", quant_gp)]
            if pt_state["free"][par] is not None:
                wv.append(("sGP", pt_state["free"][par]))
            tl = dma("scalar", tb_, src_ap, waits=wv, inc="sSC",
                     transpose=True)
            pt_state["free"][par] = dma("gpsimd", dst_ap, tb_,
                                        waits=[("sSC", tl)], inc="sGP")
        return go
    for hc in range(NHC):
        pt_jobs.append(mk_pt(w3q[:, hc * 128:(hc + 1) * 128], w3qt[hc], hc))
    for cc in range(NCC):
        pt_jobs.append(mk_pt(wo_bf[:, cc * 128:(cc + 1) * 128], woqt[cc],
                             NHC + cc))
    pt_per_head = (len(pt_jobs) + H - 1) // H

    b_loads = [None] * (H + 1)
    b_loads[0] = b_issue_loads(0, list(b_first))
    for h in range(H):
        kTb, vbb, qTb = b_bufs(h % 2)
        qld, vld = b_loads[h]
        if h + 1 < H:
            wv = [("sGP", a1_done_gp)]
            if b_prev_pe[(h + 1) % 2] is not None:
                wv.append(("sPE", b_prev_pe[(h + 1) % 2]))
            b_loads[h + 1] = b_issue_loads(h + 1, wv)
        for _ in range(pt_per_head):
            if pt_jobs:
                pt_jobs.pop(0)()
        for g in range(4):
            qs = qTb[:, g * 512:(g + 1) * 512]
            chunks = [(0, i) for i in range(4 * g + 4)] + \
                     [(1, i) for i in range(4 * g + 4)]
            n_ch = len(chunks)
            av_pe = None

            def chunk_mask(half, idx):
                if half == 0 and idx >= 4 * g:
                    return idx - 4 * g
                if half == 1 and idx >= 4 * g:
                    return 4 + (idx - 4 * g)
                return None

            sc_pe_l = [None] * n_ch

            def emit_scores(ci):
                half, idx = chunks[ci]
                par = ci % 2
                bank = 2 + par
                kcol = idx * 128 if half == 0 else 2048 + idx * 128
                wv = [("sSC", qld), ("sLD", vld)] + b_first
                if exp_done[par] is not None:
                    wv.append(("sACT", exp_done[par]))
                sc_pe_l[ci] = t_mm(P[bank][:, 0:512], kTb[:, kcol:kcol + 128],
                                   qs, True, True, waits=wv, inc="sPE")

            def emit_exp(ci):
                half, idx = chunks[ci]
                par = ci % 2
                bank = 2 + par
                wv = [("sPE", sc_pe_l[ci])]
                if avden_done[par] is not None:
                    wv.append(("sPE", avden_done[par]))
                ex = a_act(attnT[:, par * 512:par * 512 + 512],
                           P[bank][:, 0:512], AF.Exp, scale=SCALE, waits=wv)
                exp_done[par] = ex
                mid = chunk_mask(half, idx)
                if mid is not None:
                    wv = [("sACT", ex), ("sLD", consts_ld)]
                    if avden_done[par] is not None:
                        wv.append(("sPE", avden_done[par]))
                    mk = v_tt(amsk[:, par * 512:par * 512 + 512],
                              attnT[:, par * 512:par * 512 + 512],
                              masks_sb[:, mid * 512:(mid + 1) * 512],
                              ALU.mult, waits=wv)
                    mask_done[par] = mk
                    return ("sDVE", mk), amsk[:, par * 512:par * 512 + 512]
                return ("sACT", ex), attnT[:, par * 512:par * 512 + 512]

            def emit_avden(ci, gate, src):
                nonlocal av_pe, cp_prev
                half, idx = chunks[ci]
                par = ci % 2
                kcol = idx * 128 if half == 0 else 2048 + idx * 128
                first = (ci == 0)
                last = (ci == n_ch - 1)
                wv = [gate]
                if first and cp_prev is not None:
                    wv.append(("sACT", cp_prev))
                t_mm(P[4][:, 0:512], vbb[:, kcol:kcol + 128], src, first,
                     last, waits=wv)
                av_pe = t_mm(P[5][0:1, 0:512], onesb[:], src, first, last,
                             inc="sPE")
                avden_done[par] = av_pe

            gates = [None] * n_ch
            emit_scores(0)
            for ci in range(n_ch):
                if ci + 1 < n_ch:
                    emit_scores(ci + 1)
                gates[ci] = emit_exp(ci)
                emit_avden(ci, *gates[ci])
            rc = v_recip(rden[0:1, 0:512], P[5][0:1, 0:512],
                         waits=[("sPE", av_pe), ("sDVE", consts_dve)])
            bc_pe = t_mm(P[0][:, 0:512], ones1[:], rden[0:1, 0:512], True,
                         True, waits=[("sDVE", rc)], inc="sPE")
            wv = [("sPE", av_pe)]
            if nv_prev is not None:
                wv.append(("sDVE", nv_prev))
            cp_prev = a_copy(avf[:], P[4][:, 0:512], waits=wv)
            nv_prev = v_tt(avT(h, g * 512, 512), avf[:], P[0][:, 0:512],
                           ALU.mult, waits=[("sPE", bc_pe),
                                            ("sACT", cp_prev)])
        b_prev_pe[h % 2] = bc_pe
    while pt_jobs:
        pt_jobs.pop(0)()
    pt_done_gp = pl.cnt["sGP"]
    b_done = [("sDVE", pl.cnt["sDVE"]), ("sPE", pl.cnt["sPE"]),
              ("sACT", pl.cnt["sACT"])]

    if upto == "B":
        wv = list(b_done)
        for h in range(16):
            dma("gpsimd", out[h * 128:(h + 1) * 128, 0:2048],
                avT(h, 0, 2048), waits=wv, inc="sOUT")
        nopwait("sync", [("sOUT", pl.cnt["sOUT"])])
        return nc, es, pl

    c_wo_free = [None, None]
    c_stage_free = [None, None]
    c_evac_act = None
    c_lds = {}

    def c_load(k):
        jp, cc = divmod(k, NCC)
        par = cc % 2
        wob = U(0 + par * 2048, 2048)
        wv = [("sGP", pt_done_gp)]
        if k < 2:
            wv += b_done
        if c_wo_free[par] is not None:
            wv.append(("sPE", c_wo_free[par]))
        c_lds[k] = dma("scalar", wob, woqt[cc], waits=wv, inc="sSC")

    c_load(0)
    for jp in range(8):
        pe_cc = None
        for cc in range(NCC):
            par = cc % 2
            wob = U(0 + par * 2048, 2048)
            k = jp * NCC + cc
            if k + 1 < 8 * NCC:
                c_load(k + 1)
            wv = [("sSC", c_lds[k])] + (b_done if (jp == 0 and cc == 0)
                                        else [])
            if cc == 0 and c_evac_act is not None:
                wv.append(("sACT", c_evac_act))
            for sj in range(2):
                jt = 2 * jp + sj
                for oc in range(4):
                    bank = sj * 4 + oc
                    pe_cc = t_mm(P[bank][:, 0:512], avT(cc, jt * 128, 128),
                                 wob[:, oc * 512:(oc + 1) * 512],
                                 start=(cc == 0), stop=(cc == NCC - 1),
                                 waits=wv if (sj == 0 and oc == 0) else (),
                                 inc="sPE" if (sj == 1 and oc == 3) else None)
            c_wo_free[par] = pe_cc
        for sj in range(2):
            jt = 2 * jp + sj
            st = U(4096 + sj * 2048, 2048)
            cp = None
            for oc in range(4):
                bank = sj * 4 + oc
                wv = [("sPE", pe_cc)]
                if oc == 0 and c_stage_free[sj] is not None:
                    wv.append(("sGP", c_stage_free[sj]))
                cp = a_copy(st[:, oc * 512:(oc + 1) * 512], P[bank][:, 0:512],
                            waits=wv)
            c_stage_free[sj] = dma("gpsimd", sscr[jt], st,
                                   waits=[("sACT", cp)], inc="sGP")
        c_evac_act = pl.cnt["sACT"]
    c_done_gp = pl.cnt["sGP"]
    c_done_act = pl.cnt["sACT"]
    c_done_pe = pl.cnt["sPE"]

    if upto == "C":
        wv = [("sGP", c_done_gp)]
        for jt in range(16):
            dma("gpsimd", out[jt * 128:(jt + 1) * 128, 0:2048], sscr[jt],
                waits=wv, inc="sOUT")
        nopwait("sync", [("sOUT", pl.cnt["sOUT"])])
        return nc, es, pl

    d_w_free = [None, None]
    d_w3_free = [None, None]
    d_hid_free = None
    d_sig_free = [None, None]
    d_pg_act = c_done_act
    d_stile_free = None
    d_m1_prev = None
    d_ostg_free = [None, None]
    d12 = {}

    def d12_load(tb, hc):
        par = hc % 2
        w1b = U(0 + par * 2048, 2048)
        w2b = U(4096 + par * 2048, 2048)
        wv = [("sGP", quant_gp)]
        if tb == 0 and hc < 2:
            wv += [("sPE", c_done_pe), ("sGP", c_done_gp)]
        if d_w_free[par] is not None:
            wv.append(("sPE", d_w_free[par]))
        dma("scalar", w1b, w1qt[hc], waits=wv, inc="sSC")
        d12[hc] = dma("scalar", w2b, w2qt[hc], inc="sSC")

    for tb in range(4):
        px_free = [None, None, None, None]
        d12_load(tb, 0)
        for hc in range(NHC):
            par = hc % 2
            w1b = U(0 + par * 2048, 2048)
            w2b = U(4096 + par * 2048, 2048)
            if hc + 1 < NHC:
                d12_load(tb, hc + 1)
            l2 = d12[hc]
            wv = [("sSC", l2), ("sACT", d_pg_act)]
            if px_free[par] is not None:
                wv.append(px_free[par])
            if px_free[2 + par] is not None:
                wv.append(px_free[2 + par])
            for dc in range(NCC):
                t_mm(P[par][:, 0:512], w1b[:, dc * 128:(dc + 1) * 128],
                     own_xnT(dc, tb * 512, 512), start=(dc == 0),
                     stop=(dc == NCC - 1), waits=wv if dc == 0 else ())
            pe_v = None
            for dc in range(NCC):
                pe_v = t_mm(P[2 + par][:, 0:512],
                            w2b[:, dc * 128:(dc + 1) * 128],
                            own_xnT(dc, tb * 512, 512), start=(dc == 0),
                            stop=(dc == NCC - 1),
                            inc="sPE" if dc == NCC - 1 else None)
            d_w_free[par] = pe_v
            wv = [("sPE", pe_v)]
            if d_sig_free[par] is not None:
                wv.append(("sDVE", d_sig_free[par]))
            sg_a = a_act(sigf[:, par * 512:par * 512 + 512], P[par][:, 0:512],
                         AF.Sigmoid, waits=wv)
            wv = [("sACT", sg_a)]
            if d_hid_free is not None:
                wv.append(("sPE", d_hid_free))
            hv = v_tt(hid(hc, 0, 512), P[2 + par][:, 0:512],
                      sigf[:, par * 512:par * 512 + 512], ALU.mult, waits=wv)
            d_sig_free[par] = hv
            px_free[par] = ("sACT", sg_a)
            px_free[2 + par] = ("sDVE", hv)
        hid_dve = pl.cnt["sDVE"]
        if upto == "D1" and tb == 0:
            wv = [("sDVE", hid_dve)]
            for hcx in range(16):
                dma("gpsimd", out[hcx * 128:(hcx + 1) * 128, 0:512],
                    hid(hcx, 0, 512), waits=wv, inc="sOUT")
            nopwait("sync", [("sOUT", pl.cnt["sOUT"])])
            return nc, es, pl
        if upto == "D2" and tb == 0:
            w3b = U(8192, 2048)
            l3 = dma("scalar", w3b, w3q[:, 0:128], waits=[("sGP", quant_gp)],
                     inc="sSC", transpose=True)
            pl.op("gpsimd", lambda e: e.nop(), waits=[("sSC", l3)])
            dma("gpsimd", out[0:128, 0:2048], w3b, inc="sOUT")
            dma("gpsimd", out[128:256, 0:2048], w3q[1536:1664, 0:2048],
                inc="sOUT")
            dma("gpsimd", out[256:384, 0:2048], w3q[1920:2048, 3584:5632]
                if False else w3q[1920:2048, 0:2048], inc="sOUT")
            nopwait("sync", [("sOUT", pl.cnt["sOUT"])])
            return nc, es, pl
        for tsp in range(2):
            mm_done = None
            d3 = {}

            def d3_load(hc):
                par = hc % 2
                w3b = U(8192 + par * 2048, 2048)
                wv = [("sGP", pt_done_gp)]
                if tb == 0 and tsp == 0 and hc < 2:
                    wv.append(("sPE", c_done_pe))
                if d_w3_free[par] is not None:
                    wv.append(("sPE", d_w3_free[par]))
                d3[hc] = dma("scalar", w3b, w3qt[hc], waits=wv, inc="sSC")

            d3_load(0)
            for hc in range(NHC):
                par = hc % 2
                w3b = U(8192 + par * 2048, 2048)
                if hc + 1 < NHC:
                    d3_load(hc + 1)
                l3 = d3[hc]
                wv = [("sSC", l3), ("sDVE", hid_dve), ("sACT", d_pg_act)]
                pe_v = None
                for ts in range(2):
                    for oc in range(4):
                        bank = ts * 4 + oc
                        pe_v = t_mm(P[bank][:, 0:512],
                                    hid(hc, (tsp * 2 + ts) * 128, 128),
                                    w3b[:, oc * 512:(oc + 1) * 512],
                                    start=(hc == 0), stop=(hc == NHC - 1),
                                    waits=wv if (ts == 0 and oc == 0) else (),
                                    inc="sPE" if (ts == 1 and oc == 3)
                                    else None)
                d_w3_free[par] = pe_v
            mm_done = pe_v
            if upto == "D3" and tb == 0 and tsp == 0:
                for k, bk in enumerate((2, 3, 6, 7)):
                    cpd = a_copy(ostg[:, 0:512], P[bk][:, 0:512],
                                 waits=[("sPE", mm_done)])
                    dg = dma("gpsimd", out[k * 128:(k + 1) * 128, 0:512],
                             ostg[:, 0:512], waits=[("sACT", cpd)], inc="sOUT")
                    pl.op("scalar", lambda e: e.nop(),
                          waits=[("sOUT", pl.cnt["sOUT"])])
                nopwait("sync", [("sOUT", pl.cnt["sOUT"])])
                return nc, es, pl
            for ts in range(2):
                jt = tb * 4 + tsp * 2 + ts
                stile = U(12288, 2048)
                xtile = UF(7168, 2048)
                wv = [("sGP", c_done_gp)]
                if d_stile_free is not None:
                    wv.append(("sDVE", d_stile_free))
                sl = dma("sync", stile, sscr[jt], waits=wv, inc="sLD")
                xl = dma("sync", xtile, x[jt * 128:(jt + 1) * 128, :],
                         inc="sLD")
                m1 = d_m1_prev
                for oc in range(4):
                    bank = ts * 4 + oc
                    par = oc % 2
                    wv = [("sPE", mm_done)]
                    if m1 is not None:
                        wv.append(("sDVE", m1))
                    sv = a_act(gstg[:], P[bank][:, 0:512], AF.Silu, waits=wv)
                    m1 = v_tt(sgst[:], gstg[:],
                              stile[:, oc * 512:(oc + 1) * 512], ALU.mult,
                              waits=[("sACT", sv), ("sLD", sl)])
                    wv = [("sLD", xl)]
                    if d_ostg_free[par] is not None:
                        wv.append(("sOUT", d_ostg_free[par]))
                    o1 = v_tt(ostg[:, par * 512:par * 512 + 512], sgst[:],
                              xtile[:, oc * 512:(oc + 1) * 512], ALU.add,
                              waits=wv)
                    d_ostg_free[par] = dma(
                        "sync", out[jt * 128:(jt + 1) * 128,
                                    oc * 512:(oc + 1) * 512],
                        ostg[:, par * 512:par * 512 + 512],
                        waits=[("sDVE", o1)], inc="sOUT")
                d_stile_free = m1
                d_m1_prev = m1
            d_pg_act = pl.cnt["sACT"]
            d_hid_free = mm_done
    nopwait("sync", [("sOUT", pl.cnt["sOUT"])])
    return nc, es, pl


def finish_build(nc, es, pl):
    with es:
        with nc.Block() as block:
            @block.sync
            def _(e):
                for f in pl.q["sync"]:
                    f(e)

            @block.scalar
            def _(e):
                for f in pl.q["scalar"]:
                    f(e)

            @block.tensor
            def _(e):
                for f in pl.q["tensor"]:
                    f(e)

            @block.vector
            def _(e):
                for f in pl.q["vector"]:
                    f(e)

            @block.gpsimd
            def _(e):
                for f in pl.q["gpsimd"]:
                    f(e)
    return nc


def _host_tables():
    inv_freq = (1.0 / (10000.0 ** (np.arange(0, D, 2, dtype=np.float32) / D))
                ).astype(np.float32)
    t = np.arange(T, dtype=np.float32)
    freqs = t[:, None] * inv_freq[None, :]
    sin_b = np.sin(freqs).astype(np.float32)
    cos_b = np.cos(freqs).astype(np.float32)
    sin_de = np.concatenate([sin_b, sin_b], axis=1)
    cos_de = np.concatenate([cos_b, cos_b], axis=1)
    return sin_de, cos_de


def _perm_for_parity(p):
    tiles = list(range(p, 32, 2)) + list(range(1 - p, 32, 2))
    idx = np.concatenate([np.arange(g * 128, (g + 1) * 128) for g in tiles])
    return np.asarray(tiles), idx


def _masks_for_parity(p):
    tri = np.triu(np.ones((128, 128), np.float32))
    ones = np.ones((128, 128), np.float32)
    zeros = np.zeros((128, 128), np.float32)
    ms = []
    for j in range(4):
        blocks = [zeros if sct < j else (tri if sct == j else ones)
                  for sct in range(4)]
        ms.append(np.concatenate(blocks, axis=1))
    for j in range(4):
        if p == 0:
            blocks = [ones if sct > j else zeros for sct in range(4)]
        else:
            blocks = [ones if sct >= j else zeros for sct in range(4)]
        ms.append(np.concatenate(blocks, axis=1))
    return np.stack(ms).astype(ml_dtypes.bfloat16)


def _deint_qkv(wq):
    w = np.array(wq, copy=True)
    deint = np.concatenate([np.arange(0, D, 2), np.arange(1, D, 2)])
    for s in range(2):
        for h in range(H):
            base = s * C + h * D
            w[base:base + D, :] = w[base + deint, :]
    return w


_BUILT = {}


def _get_program(upto="D"):
    if upto not in _BUILT:
        nc, es, pl = build_program(upto)
        _BUILT[upto] = finish_build(nc, es, pl)
    return _BUILT[upto]


def kernel(L0_attn_x, L0_attn_W_qkv, L0_attn_W_o, L0_ffn_w1, L0_ffn_w2,
           L0_ffn_w3, _upto="D"):
    xf = np.asarray(L0_attn_x, np.float32)
    wq = _deint_qkv(np.asarray(L0_attn_W_qkv, np.float32))
    wov = np.ascontiguousarray(np.asarray(L0_attn_W_o, np.float32))
    w1v = np.ascontiguousarray(np.asarray(L0_ffn_w1, np.float32))
    w2v = np.ascontiguousarray(np.asarray(L0_ffn_w2, np.float32))
    w3v = np.ascontiguousarray(np.asarray(L0_ffn_w3, np.float32))
    ident = np.eye(128, dtype=ml_dtypes.bfloat16)
    sin_de, cos_de = _host_tables()

    in_maps = []
    metas = []
    for core in range(8):
        b, p = core // 2, core % 2
        tiles, idx = _perm_for_parity(p)
        xc = np.ascontiguousarray(xf[b][idx])
        sc = np.ascontiguousarray(
            np.stack([sin_de[idx], cos_de[idx]]).astype(np.float32))
        in_maps.append({
            "x": xc, "wqkv": wq, "wo": wov, "w1": w1v, "w2": w2v, "w3": w3v,
            "sincos": sc, "masks": _masks_for_parity(p), "ident": ident,
        })
        metas.append((b, tiles))

    nc = _get_program(_upto)
    res = run_bass_kernel_spmd(nc, in_maps, core_ids=list(range(8)))

    if _upto != "D":
        return [res.results[c]["out"] for c in range(8)]

    outp = np.empty((B, T, C), np.float32)
    for core in range(8):
        b, tiles = metas[core]
        o = res.results[core]["out"]
        for j in range(OWN):
            g = tiles[j]
            outp[b, g * 128:(g + 1) * 128, :] = o[j * 128:(j + 1) * 128, :]
    return outp


# version 11
# speedup vs baseline: 1.2140x; 1.2140x over previous
import math
import numpy as np
import ml_dtypes
from contextlib import ExitStack

import concourse.bass as bass
import concourse.mybir as mybir
from concourse.bass_utils import run_bass_kernel_spmd

F32 = mybir.dt.float32
BF16 = mybir.dt.bfloat16
AF = mybir.ActivationFunctionType
ALU = mybir.AluOpType
AX = mybir.AxisListType

B, T, C, H, D, FF = 4, 4096, 2048, 16, 128, 5632
NT = 32
OWN = 16
NCC = C // 128
NHC = FF // 128
SCALE = 1.0 / math.sqrt(D)
LN_EPS = 1e-5
RMS_EPS = 1.1920929e-07
QEPS = 1e-8

ENGINES = ("sync", "scalar", "tensor", "vector", "gpsimd")


class Pl:

    def __init__(self, sems):
        self.q = {e: [] for e in ENGINES}
        self.sems = sems
        self.cnt = {k: 0 for k in sems}
        self.last_wait = {e: {} for e in ENGINES}

    def op(self, eng, fn, waits=(), inc=None, amt=1):
        eff = []
        for wv in waits:
            if wv is None:
                continue
            name, val = wv
            if val is None or val <= 0:
                continue
            if self.last_wait[eng].get(name, -1) >= val:
                continue
            self.last_wait[eng][name] = val
            eff.append((self.sems[name], val))
        if inc is not None:
            self.cnt[inc] += amt
        h = self.sems[inc] if inc is not None else None

        def g(e, _w=tuple(eff), _fn=fn, _h=h, _amt=amt):
            for (s, v) in _w:
                e.wait_ge(s, v)
            r = _fn(e)
            if _h is not None:
                r.then_inc(_h, _amt)
        self.q[eng].append(g)
        return self.cnt[inc] if inc is not None else None


def build_program(upto="D"):
    nc = bass.Bass("TRN2", target_bir_lowering=False)

    x = nc.dram_tensor("x", [T, C], F32, kind="ExternalInput").ap()
    wqkv = nc.dram_tensor("wqkv", [3 * C, C], F32, kind="ExternalInput").ap()
    wo = nc.dram_tensor("wo", [C, C], F32, kind="ExternalInput").ap()
    w1 = nc.dram_tensor("w1", [FF, C], F32, kind="ExternalInput").ap()
    w2 = nc.dram_tensor("w2", [FF, C], F32, kind="ExternalInput").ap()
    w3 = nc.dram_tensor("w3", [C, FF], F32, kind="ExternalInput").ap()
    sincos = nc.dram_tensor("sincos", [2, T, D], F32, kind="ExternalInput").ap()
    masks = nc.dram_tensor("masks", [8, 128, 512], BF16, kind="ExternalInput").ap()
    ident = nc.dram_tensor("ident", [128, 128], BF16, kind="ExternalInput").ap()
    out = nc.dram_tensor("out", [OWN * 128, C], F32, kind="ExternalOutput").ap()

    wqkv_bf = nc.dram_tensor("wqkv_bf", [3 * C, C], BF16, kind="Internal").ap()
    wo_bf = nc.dram_tensor("wo_bf", [C, C], BF16, kind="Internal").ap()
    qkvscr = nc.dram_tensor("qkvscr", [12, NT, 128, 512], BF16, kind="Internal").ap()
    w1qt = nc.dram_tensor("w1qt", [NHC, 128, 2048], BF16, kind="Internal").ap()
    w2qt = nc.dram_tensor("w2qt", [NHC, 128, 2048], BF16, kind="Internal").ap()
    w3q = nc.dram_tensor("w3q", [C, FF], BF16, kind="Internal").ap()
    sscr = nc.dram_tensor("sscr", [OWN, 128, 2048], BF16, kind="Internal").ap()
    w3qt = nc.dram_tensor("w3qt", [NHC, 128, 2048], BF16, kind="Internal").ap()
    woqt = nc.dram_tensor("woqt", [NCC, 128, 2048], BF16, kind="Internal").ap()

    es = ExitStack()
    E = es.enter_context
    xnT_own = E(nc.sbuf_tensor("xnT_own", [128, OWN * 2048], BF16))
    pool16 = E(nc.sbuf_tensor("pool16", [128, 16 * 2048], BF16))
    util = E(nc.sbuf_tensor("util", [128, 24576], BF16))
    masks_sb = E(nc.sbuf_tensor("masks_sb", [128, 4096], BF16))
    scs = E(nc.sbuf_tensor("scs", [128, 512], F32))
    attnT = E(nc.sbuf_tensor("attnT", [128, 1536], BF16))
    amsk = E(nc.sbuf_tensor("amsk", [128, 1536], BF16))
    avf = E(nc.sbuf_tensor("avf", [128, 512], F32))
    rden = E(nc.sbuf_tensor("rden", [1, 512], F32))
    ones1 = E(nc.sbuf_tensor("ones1", [1, 128], F32))
    ones128 = E(nc.sbuf_tensor("ones128", [128, 1], F32))
    onesb = E(nc.sbuf_tensor("onesb", [128, 1], BF16))
    ident_sb = E(nc.sbuf_tensor("ident_sb", [128, 128], BF16))
    lnv = E(nc.sbuf_tensor("lnv", [128, 16], F32))
    rmsv = E(nc.sbuf_tensor("rmsv", [128, 24], F32))
    statc = E(nc.sbuf_tensor("statc", [128, 48], F32))
    stat1 = E(nc.sbuf_tensor("stat1", [128, 2], F32))
    tau2 = E(nc.sbuf_tensor("tau2", [1, 2], F32))
    thrw = E(nc.sbuf_tensor("thrw", [128, 6], F32))
    epsv = E(nc.sbuf_tensor("epsv", [128, 4], F32))

    utilf = util.ap().bitcast(F32)
    ostg = utilf[:, 9216:10240]
    sgst = utilf[:, 10240:10752]
    gstg = utilf[:, 10752:11264]
    sigf = utilf[:, 11264:12288]

    def U(c0, w):
        return util[:, c0:c0 + w]

    def UF(c0, w):
        return utilf[:, c0:c0 + w]

    P = [E(nc.psum_tensor("P%d" % i, [128, 512], F32)) for i in range(8)]
    Pb = [p.ap().bitcast(BF16) for p in P]

    sems = {}
    for name in ("sLD", "sSC", "sGP", "sOUT", "sPE", "sACT", "sDVE",
                 "dp0", "dp1", "dp2", "dp3", "dp4", "dp5", "dp6", "dp7",
                 "sT0", "sT1", "sP0", "sP1", "sO0", "sO1"):
        sems[name] = E(nc.semaphore(name))
    pl = Pl(sems)

    def dma(eng, out_ap, in_ap, waits=(), inc=None, transpose=False):
        def f(e, o=out_ap, i=in_ap, t=transpose):
            return e.dma_start(out=o, in_=i, transpose=True) if t \
                else e.dma_start(out=o, in_=i)
        return pl.op(eng, f, waits=waits, inc=inc, amt=16)

    def t_mm(out_ap, lhsT, rhs, start, stop, waits=(), inc=None):
        def f(e, o=out_ap, lt=lhsT, r=rhs, s1=start, s2=stop):
            return e.matmul(o, lt, r, start=s1, stop=s2)
        return pl.op("tensor", f, waits=waits, inc=inc)

    def t_tr(out_ap, in_ap, waits=(), inc=None):
        def f(e, o=out_ap, i=in_ap):
            return e.transpose(o, i, ident_sb[:])
        return pl.op("tensor", f, waits=waits, inc=inc)

    def a_act(out_ap, in_ap, func, scale=1.0, bias=0.0, accum=None, waits=()):
        def f(e, o=out_ap, i=in_ap, fu=func, s=scale, b=bias, a=accum):
            kw = {}
            if a is not None:
                kw["accum_out"] = a
            return e.activation(out=o, in_=i, func=fu, scale=s, bias=b, **kw)
        return pl.op("scalar", f, waits=waits, inc="sACT")

    def a_copy(out_ap, in_ap, waits=()):
        def f(e, o=out_ap, i=in_ap):
            return e.activation(out=o, in_=i, func=AF.Copy)
        return pl.op("scalar", f, waits=waits, inc="sACT")

    def v_red(out_ap, in_ap, op=ALU.add, absv=None, waits=()):
        def f(e, o=out_ap, i=in_ap, _op=op, _a=absv):
            return e.tensor_reduce(out=o, in_=i, axis=AX.X, op=_op,
                                   apply_absolute_value=_a)
        return pl.op("vector", f, waits=waits, inc="sDVE")

    def v_tt(out_ap, a_ap, b_ap, op=ALU.mult, waits=()):
        def f(e, o=out_ap, a=a_ap, b=b_ap, _op=op):
            return e.tensor_tensor(out=o, in0=a, in1=b, op=_op)
        return pl.op("vector", f, waits=waits, inc="sDVE")

    def v_stt(out_ap, in0, scalar, in1, op0, op1, accum=None, waits=()):
        def f(e, o=out_ap, a=in0, s=scalar, b=in1, o0=op0, o1=op1, ac=accum):
            kw = {}
            if ac is not None:
                kw["accum_out"] = ac
            return e.scalar_tensor_tensor(out=o, in0=a, scalar=s, in1=b,
                                          op0=o0, op1=o1, **kw)
        return pl.op("vector", f, waits=waits, inc="sDVE")

    def v_ts(out_ap, in_ap, s1, op0, waits=()):
        def f(e, o=out_ap, i=in_ap, _s=s1, _o0=op0):
            return e.tensor_scalar(out=o, in0=i, scalar1=_s, scalar2=None,
                                   op0=_o0)
        return pl.op("vector", f, waits=waits, inc="sDVE")

    def v_tsmul(out_ap, in_ap, c, waits=()):
        def f(e, o=out_ap, i=in_ap, _c=c):
            return e.tensor_scalar_mul(o, i, _c)
        return pl.op("vector", f, waits=waits, inc="sDVE")

    def v_copy(out_ap, in_ap, waits=()):
        def f(e, o=out_ap, i=in_ap):
            return e.tensor_copy(o, i)
        return pl.op("vector", f, waits=waits, inc="sDVE")

    def v_recip(out_ap, in_ap, waits=()):
        def f(e, o=out_ap, i=in_ap):
            return e.reciprocal(o, i)
        return pl.op("vector", f, waits=waits, inc="sDVE")

    def v_memset(ap_, val, waits=()):
        def f(e, o=ap_, v=val):
            return e.memset(o, v)
        return pl.op("vector", f, waits=waits, inc="sDVE")

    def nopwait(eng, waits):
        def f(e):
            return e.nop()
        return pl.op(eng, f, waits=waits)

    def own_xnT(cc, col, w=128):
        return xnT_own[:, cc * 2048 + col: cc * 2048 + col + w]

    def oth_xnT(cc, col, w=128):
        return pool16[:, cc * 2048 + col: cc * 2048 + col + w]

    def xnT_tile(cc, l):
        return own_xnT(cc, l * 128) if l < OWN else oth_xnT(cc, (l - OWN) * 128)

    def avT(h, col, w):
        return pool16[:, h * 2048 + col: h * 2048 + col + w]

    def hid(hc, col, w):
        return pool16[:, hc * 512 + col: hc * 512 + col + w]

    for mi in range(8):
        dma("sync", masks_sb[:, mi * 512:(mi + 1) * 512], masks[mi],
            inc="sLD")
    dma("sync", ident_sb[:], ident, inc="sLD")
    consts_ld = pl.cnt["sLD"]
    v_memset(ones1[:], 1.0)
    v_memset(ones128[:], 1.0)
    v_memset(onesb[:], 1.0)
    v_memset(epsv[:, 0:1], LN_EPS)
    v_memset(epsv[:, 1:2], RMS_EPS)
    v_memset(epsv[:, 2:3], 0.5 * QEPS)
    consts_dve = pl.cnt["sDVE"]

    qkv_cast = []
    for i in range(6):
        qkv_cast.append(dma("gpsimd", wqkv_bf[i * 1024:(i + 1) * 1024, :],
                            wqkv[i * 1024:(i + 1) * 1024, :], inc="sGP"))
    wo_cast = []
    for i in range(2):
        wo_cast.append(dma("gpsimd", wo_bf[i * 1024:(i + 1) * 1024, :],
                           wo[i * 1024:(i + 1) * 1024, :], inc="sGP"))

    a0_apply = [None] * NT
    a0_var = [None] * NT
    a0_evac = [[None] * NCC for _ in range(NT)]
    for l in range(NT):
        par = l % 2
        xb = UF(0 + par * 2048, 2048)
        sq = UF(4096, 2048)
        xnbf = U(12288 + par * 2048, 2048)
        lv = lnv[:, par * 8:par * 8 + 8]
        w = []
        if l >= 2:
            w.append(("sACT", a0_apply[l - 2]))
        xsem = "dp%d" % (l % 2)
        ldx = dma("sync", xb, x[l * 128:(l + 1) * 128, :], waits=w, inc=xsem)
        w = [(xsem, ldx)]
        if l >= 2:
            w.append(("sACT", a0_apply[l - 2]))
        r1 = v_red(lv[:, 0:1], xb, waits=w)
        w = [(xsem, ldx)]
        if l >= 2:
            w.append(("sDVE", a0_var[l - 2]))
        sq_a = a_act(sq, xb, AF.Square, accum=lv[:, 1:2], waits=w)
        r2 = v_tsmul(lv[:, 2:3], lv[:, 0:1], 1.0 / C, waits=[("sDVE", r1)])
        r3 = v_tt(lv[:, 3:4], lv[:, 2:3], lv[:, 2:3], ALU.mult,
                  waits=[("sDVE", r2)])
        var = v_stt(lv[:, 4:5], lv[:, 1:2], 1.0 / C, lv[:, 3:4],
                    ALU.mult, ALU.subtract,
                    waits=[("sACT", sq_a), ("sDVE", r3)])
        a0_var[l] = var
        sqv = a_act(lv[:, 5:6], lv[:, 4:5], AF.Sqrt, bias=epsv[:, 0:1],
                    waits=[("sDVE", var), ("sDVE", consts_dve)])
        rc0 = v_recip(lv[:, 6:7], lv[:, 5:6], waits=[("sACT", sqv)])
        nmr = v_stt(lv[:, 7:8], lv[:, 2:3], -1.0, lv[:, 6:7], ALU.mult,
                    ALU.mult, waits=[("sDVE", rc0)])
        w = [("sDVE", nmr)]
        if l >= 2:
            w.append(a0_evac[l - 2][NCC - 1])
        ap_a = a_act(xnbf, xb, AF.Identity, scale=lv[:, 6:7], bias=lv[:, 7:8],
                     waits=w)
        a0_apply[l] = ap_a
        dcol = l * 128 if l < OWN else (l - OWN) * 128
        dest = own_xnT if l < OWN else oth_xnT
        for cc in range(NCC):
            bank = 6 + (cc % 2)
            w = [("sACT", ap_a), ("sLD", consts_ld)]
            if cc >= 2:
                w.append(a0_evac[l][cc - 2])
            elif l >= 1:
                w.append(a0_evac[l - 1][NCC - 2 + cc])
            pe_v = t_tr(Pb[bank][:, 0:128], xnbf[:, cc * 128:(cc + 1) * 128],
                        waits=w, inc="sPE")
            if cc % 2 == 0:
                cnt = v_copy(dest(cc, dcol), Pb[bank][:, 0:128],
                             waits=[("sPE", pe_v)])
                a0_evac[l][cc] = ("sDVE", cnt)
            else:
                cnt = a_copy(dest(cc, dcol), Pb[bank][:, 0:128],
                             waits=[("sPE", pe_v)])
                a0_evac[l][cc] = ("sACT", cnt)
    a0_done = [("sACT", pl.cnt["sACT"]), ("sDVE", pl.cnt["sDVE"])]

    if upto == "A0":
        wv = list(a0_done)
        for cc in range(16):
            dma("gpsimd", out[cc * 128:(cc + 1) * 128, 0:2048],
                own_xnT(cc, 0, 2048), waits=wv, inc="sOUT")
        nopwait("sync", [("sOUT", pl.cnt["sOUT"])])
        return nc, es, pl

    wT = U(0, 8192)

    def wT_tile(cc):
        return wT[:, cc * 512:(cc + 1) * 512]

    q2p = [UF(7680, 512), UF(9728, 512)]
    tmp2p = [UF(8192, 512), UF(10240, 512)]
    sqsp = [UF(8704, 512), UF(10752, 512)]
    q3p = [UF(9216, 512), UF(11264, 512)]
    a1_mm_done = None
    a1_bank_free = [None, None]
    a1_stage_free = [None, None]
    a1_scs_free = [None, None]
    a1_q2_free = [None, None]
    cast_total = pl.cnt["sGP"]
    for osl in range(12):
        wv = [("sGP", cast_total)] + a0_done
        if a1_mm_done is not None:
            wv.append(("sPE", a1_mm_done))
        wld = None
        for cc in range(NCC):
            src = wqkv_bf[osl * 512:(osl + 1) * 512, cc * 128:(cc + 1) * 128]
            wld = dma("scalar", wT_tile(cc), src, waits=wv if cc == 0 else (),
                      inc="sSC", transpose=True)
        lrange = range(OWN) if osl < 4 else range(NT)
        for l in lrange:
            par = l % 2
            bank = par
            is_v = osl >= 8
            scl = None
            ssem = "dp%d" % (2 + par)
            if not is_v:
                wv = []
                if a1_scs_free[par] is not None:
                    wv.append(("sDVE", a1_scs_free[par]))
                dma("sync", scs[:, par * 256:par * 256 + 128],
                    sincos[0, l * 128:(l + 1) * 128, :], waits=wv, inc=ssem)
                scl = dma("sync", scs[:, par * 256 + 128:par * 256 + 256],
                          sincos[1, l * 128:(l + 1) * 128, :], inc=ssem)
            wv = [("sSC", wld)] + a0_done
            if a1_bank_free[par] is not None:
                wv.append(a1_bank_free[par])
            pe_v = None
            for cc in range(NCC):
                pe_v = t_mm(P[bank][:, 0:512], xnT_tile(cc, l), wT_tile(cc),
                            start=(cc == 0), stop=(cc == NCC - 1),
                            waits=wv if cc == 0 else (),
                            inc="sPE" if cc == NCC - 1 else None)
            a1_mm_done = pe_v
            st = U(23552 + par * 512, 512)
            if is_v:
                w2v = [("sPE", pe_v)]
                if a1_stage_free[par] is not None:
                    w2v.append(("sT%d" % par, a1_stage_free[par]))
                cp = a_copy(st, P[bank][:, 0:512], waits=w2v)
                a1_bank_free[par] = ("sACT", cp)
                a1_stage_free[par] = dma("gpsimd", qkvscr[osl, l], st,
                                         waits=[("sACT", cp)],
                                         inc="sT%d" % par)
            else:
                q2 = q2p[par]
                tmp2 = tmp2p[par]
                sqs = sqsp[par]
                q3 = q3p[par]
                rv = rmsv[:, par * 12:par * 12 + 12]
                sinb = scs[:, par * 256:par * 256 + 64]
                cosb = scs[:, par * 256 + 128:par * 256 + 256]
                sin_b = sinb.unsqueeze(1).broadcast_to([128, 4, 64])
                cos_b = cosb.unsqueeze(1).broadcast_to([128, 4, 128])
                pk3 = P[bank][:, 0:512].rearrange("p (h z) -> p h z", h=4)
                q23 = q2.rearrange("p (h z) -> p h z", h=4)
                wv = [("sPE", pe_v), (ssem, scl)]
                if a1_q2_free[par] is not None:
                    wv.append(("sACT", a1_q2_free[par]))
                v_stt(q23[:, :, 0:64], pk3[:, :, 64:128], -1.0, sin_b,
                      ALU.mult, ALU.mult, waits=wv)
                v_stt(q23[:, :, 64:128], pk3[:, :, 0:64], 1.0, sin_b,
                      ALU.mult, ALU.mult)
                cosm = v_tt(tmp2.rearrange("p (h z) -> p h z", h=4),
                            pk3, cos_b, ALU.mult)
                a1_bank_free[par] = ("sDVE", cosm)
                a1_scs_free[par] = cosm
                ad = v_tt(q3, q2, tmp2, ALU.add, waits=[("sDVE", cosm)])
                for hh in range(4):
                    v_stt(sqs[:, hh * 128:(hh + 1) * 128],
                          q3[:, hh * 128:(hh + 1) * 128], 1.0,
                          q3[:, hh * 128:(hh + 1) * 128], ALU.mult, ALU.mult,
                          accum=rv[:, hh:hh + 1],
                          waits=[("sDVE", ad)] if hh == 0 else ())
                dv = pl.cnt["sDVE"]
                sr = a_act(rv[:, 4:8], rv[:, 0:4], AF.Sqrt, scale=1.0 / D,
                           bias=epsv[:, 1:2],
                           waits=[("sDVE", dv), ("sDVE", consts_dve)])
                rc = v_recip(rv[:, 8:12], rv[:, 4:8], waits=[("sACT", sr)])
                w2v = [("sDVE", rc)]
                if a1_stage_free[par] is not None:
                    w2v.append(("sT%d" % par, a1_stage_free[par]))
                cp = None
                for hh in range(4):
                    cp = a_act(st[:, hh * 128:(hh + 1) * 128],
                               q3[:, hh * 128:(hh + 1) * 128], AF.Identity,
                               scale=rv[:, 8 + hh:9 + hh],
                               waits=w2v if hh == 0 else ())
                a1_q2_free[par] = cp
                a1_stage_free[par] = dma("gpsimd", qkvscr[osl, l], st,
                                         waits=[("sACT", cp)],
                                         inc="sT%d" % par)
    a1_scr_done = [("sT0", pl.cnt["sT0"]), ("sT1", pl.cnt["sT1"])]
    a1_done = [("sACT", pl.cnt["sACT"]), ("sDVE", pl.cnt["sDVE"]),
               ("sPE", pl.cnt["sPE"])]

    if upto == "A1":
        wv = list(a1_scr_done)
        for i, (osl, l) in enumerate([(0, 0), (0, 15), (5, 0), (5, 17),
                                      (9, 0), (9, 17), (3, 15), (7, 31)]):
            dma("gpsimd", out[i * 128:(i + 1) * 128, 0:512],
                qkvscr[osl, l], waits=wv, inc="sOUT")
        nopwait("sync", [("sOUT", pl.cnt["sOUT"])])
        return nc, es, pl

    wflat = {
        "w1": w1.rearrange("(r p) c -> r p c", p=128),
        "w2": w2.rearrange("(r p) c -> r p c", p=128),
        "w3": w3.flatten().rearrange("(r p c) -> r p c", p=128, c=2048),
    }
    NBLK = 44
    NW = float(FF * C)
    wf_free = [None, None]
    ta_prev = None
    cp3_prev = None
    for wi, wname in enumerate(("w1", "w2", "w3")):
        for r in range(NBLK):
            par = r % 2
            wf = UF(5120 + par * 2048, 2048)
            wv = ([wf_free[par]] if wf_free[par] is not None else []) + \
                (a1_done if (wi == 0 and r < 2) else [])
            ld = dma("sync", wf, wflat[wname][r], waits=wv,
                     inc="dp%d" % par)
            dv = v_red(statc[:, r:r + 1], wf, absv=True,
                       waits=[("dp%d" % par, ld)])
            wf_free[par] = ("sDVE", dv)
        dv = v_red(stat1[:, 0:1], statc[:, 0:NBLK], waits=[("sDVE", dv)])
        wv = [("sDVE", dv), ("sDVE", consts_dve)]
        if ta_prev is not None:
            wv.append(("sACT", ta_prev))
        pe_v = t_mm(P[2][0:1, 0:1], ones128[:], stat1[:, 0:1], True, True,
                    waits=wv, inc="sPE")
        ta = a_act(tau2[0:1, 0:1], P[2][0:1, 0:1], AF.Identity,
                   scale=0.5 / NW, bias=epsv[0:1, 2:3],
                   waits=[("sPE", pe_v), ("sDVE", consts_dve)])
        ta_prev = ta
        nt_ = v_tsmul(tau2[0:1, 1:2], tau2[0:1, 0:1], -1.0,
                      waits=[("sACT", ta)])
        wv = [("sDVE", nt_)]
        if cp3_prev is not None:
            wv.append(("sDVE", cp3_prev))
        pe_v = t_mm(P[3][:, 0:2], ones1[:], tau2[0:1, 0:2], True, True,
                    waits=wv, inc="sPE")
        cp3_prev = v_copy(thrw[:, wi * 2:wi * 2 + 2], P[3][:, 0:2],
                          waits=[("sPE", pe_v)])
    thr_dve = pl.cnt["sDVE"]
    stats_act = pl.cnt["sACT"]

    qt_free = [None, None]
    st_free = [None, None]
    wf_free = [None, None]
    cp_prev_hc = stats_act
    for wi, wsrc, wdst in ((0, w1, w1qt), (1, w2, w2qt)):
        for hc in range(NHC):
            par = hc % 2
            wf = UF(5120 + par * 2048, 2048)
            qt = U(0 + par * 2048, 2048)
            st = U(4096 + par * 2048, 2048)
            wv = [wf_free[par]] if wf_free[par] is not None else []
            ld = dma("sync", wf, wsrc[hc * 128:(hc + 1) * 128, :], waits=wv,
                     inc="dp%d" % par)
            wv = [("dp%d" % par, ld), ("sDVE", thr_dve)] + \
                (a1_done if hc < 2 else [])
            if qt_free[par] is not None:
                wv.append(qt_free[par])
            v_ts(qt, wf, thrw[:, wi * 2:wi * 2 + 1], ALU.is_gt, waits=wv)
            lt = v_ts(U(8192, 2048), wf, thrw[:, wi * 2 + 1:wi * 2 + 2],
                      ALU.is_lt)
            qd = v_tt(qt, qt, U(8192, 2048), ALU.subtract,
                      waits=[("sDVE", lt)])
            wf_free[par] = ("sDVE", qd)
            cp = cp_prev_hc
            for g4 in range(4):
                bank = 6 + (g4 % 2)
                wvt = [("sDVE", qd)]
                if cp is not None:
                    wvt.append(("sACT", cp))
                pe_v = None
                for k in range(4):
                    dc = g4 * 4 + k
                    pe_v = t_tr(Pb[bank][:, k * 128:(k + 1) * 128],
                                qt[:, dc * 128:(dc + 1) * 128],
                                waits=wvt if k == 0 else (),
                                inc="sPE" if k == 3 else None)
                wvc = [("sPE", pe_v)]
                if g4 == 0 and st_free[par] is not None:
                    wvc.append(("sT%d" % par, st_free[par]))
                cp = a_copy(st[:, g4 * 512:(g4 + 1) * 512], Pb[bank][:, 0:512],
                            waits=wvc)
            qt_free[par] = ("sACT", cp)
            cp_prev_hc = cp
            st_free[par] = dma("gpsimd", wdst[hc], st, waits=[("sACT", cp)],
                               inc="sT%d" % par)
    for r in range(16):
        for ci, cw in ((0, 2048), (2048, 2048), (4096, 1536)):
            par = (r * 3 + ci // 2048) % 2
            wf = UF(5120 + par * 2048, cw)
            qt = U(0 + par * 2048, cw)
            wv = [wf_free[par]] if wf_free[par] is not None else []
            ld = dma("sync", wf, w3[r * 128:(r + 1) * 128, ci:ci + cw],
                     waits=wv, inc="dp%d" % par)
            wv = [("dp%d" % par, ld), ("sDVE", thr_dve)]
            if qt_free[par] is not None:
                wv.append(qt_free[par])
            v_ts(qt, wf, thrw[:, 4:5], ALU.is_gt, waits=wv)
            lt = v_ts(U(8192, cw), wf, thrw[:, 5:6], ALU.is_lt)
            qd = v_tt(qt, qt, U(8192, cw), ALU.subtract,
                      waits=[("sDVE", lt)])
            wf_free[par] = ("sDVE", qd)
            gp = dma("gpsimd", w3q[r * 128:(r + 1) * 128, ci:ci + cw], qt,
                     waits=[("sDVE", qd)], inc="sT%d" % par)
            qt_free[par] = ("sT%d" % par, gp)
    quant_scr = [("sT0", pl.cnt["sT0"]), ("sT1", pl.cnt["sT1"])]
    quant_act = pl.cnt["sACT"]
    quant_dve = pl.cnt["sDVE"]
    quant_pe = pl.cnt["sPE"]

    if upto == "QUANT":
        wv = list(quant_scr)
        dma("gpsimd", out[0:128, 0:2048], w1qt[3], waits=wv, inc="sOUT")
        dma("gpsimd", out[128:256, 0:2048], w2qt[40], waits=wv, inc="sOUT")
        dma("gpsimd", out[256:384, 0:2048], w3q[512:640, 1024:3072], waits=wv,
            inc="sOUT")
        nopwait("sync", [("sOUT", pl.cnt["sOUT"])])
        return nc, es, pl

    def b_bufs(hp):
        if hp == 0:
            return U(0, 4096), U(4096, 4096), U(8192, 2048)
        return U(10240, 4096), U(14336, 4096), U(18432, 2048)

    def b_issue_loads(h, wv):
        kTb, vbb, qTb = b_bufs(h % 2)
        os_q, os_k, os_v = h // 4, 4 + h // 4, 8 + h // 4
        dcol = (h % 4) * 128
        ksem = "dp%d" % (4 + h % 2)
        vsem = "dp%d" % (2 + h % 2)
        ksl = qkvscr[os_k][:, :, dcol:dcol + 128].rearrange("l t d -> (l t) d")
        kld = dma("scalar", kTb, ksl, waits=wv, inc=ksem, transpose=True)
        qld = dma("scalar", qTb, qkvscr[os_q][0:OWN][:, :, dcol:dcol + 128]
                  .rearrange("l t d -> (l t) d"), inc=ksem, transpose=True)
        vld = None
        for l in range(NT):
            vld = dma("sync", vbb[:, l * 128:(l + 1) * 128],
                      qkvscr[os_v, l][:, dcol:dcol + 128],
                      waits=wv if l == 0 else (), inc=vsem)
        return (ksem, qld), (vsem, vld)

    b_prev_pe = [None, None]
    exp_done = [None, None, None]
    mask_done = [None, None, None]
    avden_done = [None, None, None]
    SBK = (2, 3, 6)
    cp_prev = None
    nv_prev = None
    b_first = [("sACT", max(stats_act, quant_act)), ("sDVE", thr_dve),
               ("sDVE", quant_dve), ("sPE", quant_pe)] + \
        a1_scr_done + quant_scr
    pt_jobs = []
    pt_state = {"free": [None, None], "done": [0]}

    def mk_pt(src_ap, dst_ap, idx):
        def go():
            par = idx % 2
            tb_ = U(20480 + par * 2048, 2048)
            wv = list(quant_scr)
            if pt_state["free"][par] is not None:
                wv.append(("sP%d" % par, pt_state["free"][par]))
            tl = dma("scalar", tb_, src_ap, waits=wv, inc="dp%d" % (6 + par),
                     transpose=True)
            pt_state["free"][par] = dma("gpsimd", dst_ap, tb_,
                                        waits=[("dp%d" % (6 + par), tl)],
                                        inc="sP%d" % par)
        return go
    for hc in range(NHC):
        pt_jobs.append(mk_pt(w3q[:, hc * 128:(hc + 1) * 128], w3qt[hc], hc))
    for cc in range(NCC):
        pt_jobs.append(mk_pt(wo_bf[:, cc * 128:(cc + 1) * 128], woqt[cc],
                             NHC + cc))
    pt_per_head = (len(pt_jobs) + H - 1) // H

    b_loads = [None] * (H + 1)
    b_loads[0] = b_issue_loads(0, list(b_first))
    for h in range(H):
        kTb, vbb, qTb = b_bufs(h % 2)
        qld, vld = b_loads[h]
        if h + 1 < H:
            wv = list(a1_scr_done)
            if b_prev_pe[(h + 1) % 2] is not None:
                wv.append(("sPE", b_prev_pe[(h + 1) % 2]))
            b_loads[h + 1] = b_issue_loads(h + 1, wv)
        for _ in range(pt_per_head):
            if pt_jobs:
                pt_jobs.pop(0)()
        for g in range(4):
            qs = qTb[:, g * 512:(g + 1) * 512]
            chunks = [(0, i) for i in range(4 * g + 4)] + \
                     [(1, i) for i in range(4 * g + 4)]
            n_ch = len(chunks)
            av_pe = None

            def chunk_mask(half, idx):
                if half == 0 and idx >= 4 * g:
                    return idx - 4 * g
                if half == 1 and idx >= 4 * g:
                    return 4 + (idx - 4 * g)
                return None

            sc_pe_l = [None] * n_ch

            def emit_scores(ci):
                half, idx = chunks[ci]
                par = ci % 3
                bank = SBK[par]
                kcol = idx * 128 if half == 0 else 2048 + idx * 128
                wv = [qld, vld] + b_first
                if exp_done[par] is not None:
                    wv.append(("sACT", exp_done[par]))
                sc_pe_l[ci] = t_mm(P[bank][:, 0:512], kTb[:, kcol:kcol + 128],
                                   qs, True, True, waits=wv, inc="sPE")

            def emit_exp(ci):
                half, idx = chunks[ci]
                par = ci % 3
                bank = SBK[par]
                wv = [("sPE", sc_pe_l[ci])]
                if avden_done[par] is not None:
                    wv.append(("sPE", avden_done[par]))
                ex = a_act(attnT[:, par * 512:par * 512 + 512],
                           P[bank][:, 0:512], AF.Exp, scale=SCALE, waits=wv)
                exp_done[par] = ex
                mid = chunk_mask(half, idx)
                if mid is not None:
                    wv = [("sACT", ex), ("sLD", consts_ld)]
                    if avden_done[par] is not None:
                        wv.append(("sPE", avden_done[par]))
                    mk = v_tt(amsk[:, par * 512:par * 512 + 512],
                              attnT[:, par * 512:par * 512 + 512],
                              masks_sb[:, mid * 512:(mid + 1) * 512],
                              ALU.mult, waits=wv)
                    mask_done[par] = mk
                    return ("sDVE", mk), amsk[:, par * 512:par * 512 + 512]
                return ("sACT", ex), attnT[:, par * 512:par * 512 + 512]

            def emit_avden(ci, gate, src):
                nonlocal av_pe, cp_prev
                half, idx = chunks[ci]
                par = ci % 3
                kcol = idx * 128 if half == 0 else 2048 + idx * 128
                first = (ci == 0)
                last = (ci == n_ch - 1)
                wv = [gate]
                if first and cp_prev is not None:
                    wv.append(("sACT", cp_prev))
                t_mm(P[4][:, 0:512], vbb[:, kcol:kcol + 128], src, first,
                     last, waits=wv)
                av_pe = t_mm(P[5][0:1, 0:512], onesb[:], src, first, last,
                             inc="sPE")
                avden_done[par] = av_pe

            gates = [None] * n_ch
            emit_scores(0)
            if n_ch > 1:
                emit_scores(1)
            for ci in range(n_ch):
                if ci + 2 < n_ch:
                    emit_scores(ci + 2)
                gates[ci] = emit_exp(ci)
                emit_avden(ci, *gates[ci])
            rc = v_recip(rden[0:1, 0:512], P[5][0:1, 0:512],
                         waits=[("sPE", av_pe), ("sDVE", consts_dve)])
            bc_pe = t_mm(P[0][:, 0:512], ones1[:], rden[0:1, 0:512], True,
                         True, waits=[("sDVE", rc)], inc="sPE")
            wv = [("sPE", av_pe)]
            if nv_prev is not None:
                wv.append(("sDVE", nv_prev))
            cp_prev = a_copy(avf[:], P[4][:, 0:512], waits=wv)
            nv_prev = v_tt(avT(h, g * 512, 512), avf[:], P[0][:, 0:512],
                           ALU.mult, waits=[("sPE", bc_pe),
                                            ("sACT", cp_prev)])
        b_prev_pe[h % 2] = bc_pe
    while pt_jobs:
        pt_jobs.pop(0)()
    pt_done = [("sP0", pl.cnt["sP0"]), ("sP1", pl.cnt["sP1"])]
    b_done = [("sDVE", pl.cnt["sDVE"]), ("sPE", pl.cnt["sPE"]),
              ("sACT", pl.cnt["sACT"])]

    if upto == "B":
        wv = list(b_done)
        for h in range(16):
            dma("gpsimd", out[h * 128:(h + 1) * 128, 0:2048],
                avT(h, 0, 2048), waits=wv, inc="sOUT")
        nopwait("sync", [("sOUT", pl.cnt["sOUT"])])
        return nc, es, pl

    c_wo_free = [None, None]
    c_stage_free = [None, None]
    c_evac_act = None
    c_lds = {}

    def c_load(k):
        jp, cc = divmod(k, NCC)
        par = cc % 2
        wob = U(0 + par * 2048, 2048)
        wv = list(pt_done)
        if k < 2:
            wv += b_done
        if c_wo_free[par] is not None:
            wv.append(("sPE", c_wo_free[par]))
        c_lds[k] = dma("scalar", wob, woqt[cc], waits=wv, inc="dp%d" % par)

    c_load(0)
    for jp in range(8):
        pe_cc = None
        for cc in range(NCC):
            par = cc % 2
            wob = U(0 + par * 2048, 2048)
            k = jp * NCC + cc
            if k + 1 < 8 * NCC:
                c_load(k + 1)
            wv = [("dp%d" % par, c_lds[k])] + (b_done if (jp == 0 and
                                                          cc == 0) else [])
            if cc == 0 and c_evac_act is not None:
                wv.append(("sACT", c_evac_act))
            for sj in range(2):
                jt = 2 * jp + sj
                for oc in range(4):
                    bank = sj * 4 + oc
                    pe_cc = t_mm(P[bank][:, 0:512], avT(cc, jt * 128, 128),
                                 wob[:, oc * 512:(oc + 1) * 512],
                                 start=(cc == 0), stop=(cc == NCC - 1),
                                 waits=wv if (sj == 0 and oc == 0) else (),
                                 inc="sPE" if (sj == 1 and oc == 3) else None)
            c_wo_free[par] = pe_cc
        for sj in range(2):
            jt = 2 * jp + sj
            st = U(4096 + sj * 2048, 2048)
            cp = None
            for oc in range(4):
                bank = sj * 4 + oc
                wv = [("sPE", pe_cc)]
                if oc == 0 and c_stage_free[sj] is not None:
                    wv.append(("sT%d" % sj, c_stage_free[sj]))
                cp = a_copy(st[:, oc * 512:(oc + 1) * 512], P[bank][:, 0:512],
                            waits=wv)
            c_stage_free[sj] = dma("gpsimd", sscr[jt], st,
                                   waits=[("sACT", cp)], inc="sT%d" % sj)
        c_evac_act = pl.cnt["sACT"]
    c_scr_done = [("sT0", pl.cnt["sT0"]), ("sT1", pl.cnt["sT1"])]
    c_done_act = pl.cnt["sACT"]
    c_done_pe = pl.cnt["sPE"]

    if upto == "C":
        wv = list(c_scr_done)
        for jt in range(16):
            dma("gpsimd", out[jt * 128:(jt + 1) * 128, 0:2048], sscr[jt],
                waits=wv, inc="sOUT")
        nopwait("sync", [("sOUT", pl.cnt["sOUT"])])
        return nc, es, pl

    d_w_free = [None, None]
    d_w3_free = [None, None]
    d_hid_free = None
    d_sig_free = [None, None]
    d_pg_act = c_done_act
    d_stile_free = None
    d_m1_prev = None
    d_ostg_free = [None, None]
    d12 = {}

    def d12_load(tb, hc):
        par = hc % 2
        w1b = U(0 + par * 2048, 2048)
        w2b = U(4096 + par * 2048, 2048)
        wv = list(quant_scr)
        if tb == 0 and hc < 2:
            wv += [("sPE", c_done_pe)] + c_scr_done
        if d_w_free[par] is not None:
            wv.append(("sPE", d_w_free[par]))
        dma("scalar", w1b, w1qt[hc], waits=wv, inc="dp%d" % (2 + par))
        d12[hc] = dma("scalar", w2b, w2qt[hc], inc="dp%d" % (2 + par))

    for tb in range(4):
        px_free = [None, None, None, None]
        d12_load(tb, 0)
        for hc in range(NHC):
            par = hc % 2
            w1b = U(0 + par * 2048, 2048)
            w2b = U(4096 + par * 2048, 2048)
            if hc + 1 < NHC:
                d12_load(tb, hc + 1)
            l2 = d12[hc]
            wv = [("dp%d" % (2 + par), l2), ("sACT", d_pg_act)]
            if px_free[par] is not None:
                wv.append(px_free[par])
            if px_free[2 + par] is not None:
                wv.append(px_free[2 + par])
            for dc in range(NCC):
                t_mm(P[par][:, 0:512], w1b[:, dc * 128:(dc + 1) * 128],
                     own_xnT(dc, tb * 512, 512), start=(dc == 0),
                     stop=(dc == NCC - 1), waits=wv if dc == 0 else ())
            pe_v = None
            for dc in range(NCC):
                pe_v = t_mm(P[2 + par][:, 0:512],
                            w2b[:, dc * 128:(dc + 1) * 128],
                            own_xnT(dc, tb * 512, 512), start=(dc == 0),
                            stop=(dc == NCC - 1),
                            inc="sPE" if dc == NCC - 1 else None)
            d_w_free[par] = pe_v
            wv = [("sPE", pe_v)]
            if d_sig_free[par] is not None:
                wv.append(("sDVE", d_sig_free[par]))
            sg_a = a_act(sigf[:, par * 512:par * 512 + 512], P[par][:, 0:512],
                         AF.Sigmoid, waits=wv)
            wv = [("sACT", sg_a)]
            if d_hid_free is not None:
                wv.append(("sPE", d_hid_free))
            hv = v_tt(hid(hc, 0, 512), P[2 + par][:, 0:512],
                      sigf[:, par * 512:par * 512 + 512], ALU.mult, waits=wv)
            d_sig_free[par] = hv
            px_free[par] = ("sACT", sg_a)
            px_free[2 + par] = ("sDVE", hv)
        hid_dve = pl.cnt["sDVE"]
        if upto == "D1" and tb == 0:
            wv = [("sDVE", hid_dve)]
            for hcx in range(16):
                dma("gpsimd", out[hcx * 128:(hcx + 1) * 128, 0:512],
                    hid(hcx, 0, 512), waits=wv, inc="sOUT")
            nopwait("sync", [("sOUT", pl.cnt["sOUT"])])
            return nc, es, pl
        if upto == "D2" and tb == 0:
            w3b = U(8192, 2048)
            l3 = dma("scalar", w3b, w3q[:, 0:128], waits=list(quant_scr),
                     inc="sSC", transpose=True)
            pl.op("gpsimd", lambda e: e.nop(), waits=[("sSC", l3)])
            dma("gpsimd", out[0:128, 0:2048], w3b, inc="sOUT")
            dma("gpsimd", out[128:256, 0:2048], w3q[1536:1664, 0:2048],
                inc="sOUT")
            dma("gpsimd", out[256:384, 0:2048], w3q[1920:2048, 3584:5632]
                if False else w3q[1920:2048, 0:2048], inc="sOUT")
            nopwait("sync", [("sOUT", pl.cnt["sOUT"])])
            return nc, es, pl
        for tsp in range(2):
            mm_done = None
            d3 = {}

            def d3_load(hc):
                par = hc % 2
                w3b = U(8192 + par * 2048, 2048)
                wv = list(pt_done)
                if tb == 0 and tsp == 0 and hc < 2:
                    wv.append(("sPE", c_done_pe))
                if d_w3_free[par] is not None:
                    wv.append(("sPE", d_w3_free[par]))
                d3[hc] = dma("scalar", w3b, w3qt[hc], waits=wv,
                             inc="dp%d" % (4 + par))

            d3_load(0)
            for hc in range(NHC):
                par = hc % 2
                w3b = U(8192 + par * 2048, 2048)
                if hc + 1 < NHC:
                    d3_load(hc + 1)
                l3 = d3[hc]
                wv = [("dp%d" % (4 + par), l3), ("sDVE", hid_dve),
                      ("sACT", d_pg_act)]
                pe_v = None
                for ts in range(2):
                    for oc in range(4):
                        bank = ts * 4 + oc
                        pe_v = t_mm(P[bank][:, 0:512],
                                    hid(hc, (tsp * 2 + ts) * 128, 128),
                                    w3b[:, oc * 512:(oc + 1) * 512],
                                    start=(hc == 0), stop=(hc == NHC - 1),
                                    waits=wv if (ts == 0 and oc == 0) else (),
                                    inc="sPE" if (ts == 1 and oc == 3)
                                    else None)
                d_w3_free[par] = pe_v
            mm_done = pe_v
            if upto == "D3" and tb == 0 and tsp == 0:
                for k, bk in enumerate((2, 3, 6, 7)):
                    cpd = a_copy(ostg[:, 0:512], P[bk][:, 0:512],
                                 waits=[("sPE", mm_done)])
                    dg = dma("gpsimd", out[k * 128:(k + 1) * 128, 0:512],
                             ostg[:, 0:512], waits=[("sACT", cpd)], inc="sOUT")
                    pl.op("scalar", lambda e: e.nop(),
                          waits=[("sOUT", pl.cnt["sOUT"])])
                nopwait("sync", [("sOUT", pl.cnt["sOUT"])])
                return nc, es, pl
            for ts in range(2):
                jt = tb * 4 + tsp * 2 + ts
                stile = U(12288, 2048)
                xtile = UF(7168, 2048)
                wv = list(c_scr_done) + list(pt_done)
                if d_stile_free is not None:
                    wv.append(("sDVE", d_stile_free))
                sl = dma("sync", stile, sscr[jt], waits=wv, inc="dp6")
                xl = dma("sync", xtile, x[jt * 128:(jt + 1) * 128, :],
                         inc="dp7")
                m1 = d_m1_prev
                for oc in range(4):
                    bank = ts * 4 + oc
                    par = oc % 2
                    wv = [("sPE", mm_done)]
                    if m1 is not None:
                        wv.append(("sDVE", m1))
                    sv = a_act(gstg[:], P[bank][:, 0:512], AF.Silu, waits=wv)
                    m1 = v_tt(sgst[:], gstg[:],
                              stile[:, oc * 512:(oc + 1) * 512], ALU.mult,
                              waits=[("sACT", sv), ("dp6", sl)])
                    wv = [("dp7", xl)]
                    if d_ostg_free[par] is not None:
                        wv.append(("sO%d" % par, d_ostg_free[par]))
                    o1 = v_tt(ostg[:, par * 512:par * 512 + 512], sgst[:],
                              xtile[:, oc * 512:(oc + 1) * 512], ALU.add,
                              waits=wv)
                    d_ostg_free[par] = dma(
                        "sync", out[jt * 128:(jt + 1) * 128,
                                    oc * 512:(oc + 1) * 512],
                        ostg[:, par * 512:par * 512 + 512],
                        waits=[("sDVE", o1)], inc="sO%d" % par)
                d_stile_free = m1
                d_m1_prev = m1
            d_pg_act = pl.cnt["sACT"]
            d_hid_free = mm_done
    nopwait("sync", [("sO0", pl.cnt["sO0"]), ("sO1", pl.cnt["sO1"])])
    return nc, es, pl


def finish_build(nc, es, pl):
    with es:
        with nc.Block() as block:
            @block.sync
            def _(e):
                for f in pl.q["sync"]:
                    f(e)

            @block.scalar
            def _(e):
                for f in pl.q["scalar"]:
                    f(e)

            @block.tensor
            def _(e):
                for f in pl.q["tensor"]:
                    f(e)

            @block.vector
            def _(e):
                for f in pl.q["vector"]:
                    f(e)

            @block.gpsimd
            def _(e):
                for f in pl.q["gpsimd"]:
                    f(e)
    return nc


def _host_tables():
    inv_freq = (1.0 / (10000.0 ** (np.arange(0, D, 2, dtype=np.float32) / D))
                ).astype(np.float32)
    t = np.arange(T, dtype=np.float32)
    freqs = t[:, None] * inv_freq[None, :]
    sin_b = np.sin(freqs).astype(np.float32)
    cos_b = np.cos(freqs).astype(np.float32)
    sin_de = np.concatenate([sin_b, sin_b], axis=1)
    cos_de = np.concatenate([cos_b, cos_b], axis=1)
    return sin_de, cos_de


def _perm_for_parity(p):
    tiles = list(range(p, 32, 2)) + list(range(1 - p, 32, 2))
    idx = np.concatenate([np.arange(g * 128, (g + 1) * 128) for g in tiles])
    return np.asarray(tiles), idx


def _masks_for_parity(p):
    tri = np.triu(np.ones((128, 128), np.float32))
    ones = np.ones((128, 128), np.float32)
    zeros = np.zeros((128, 128), np.float32)
    ms = []
    for j in range(4):
        blocks = [zeros if sct < j else (tri if sct == j else ones)
                  for sct in range(4)]
        ms.append(np.concatenate(blocks, axis=1))
    for j in range(4):
        if p == 0:
            blocks = [ones if sct > j else zeros for sct in range(4)]
        else:
            blocks = [ones if sct >= j else zeros for sct in range(4)]
        ms.append(np.concatenate(blocks, axis=1))
    return np.stack(ms).astype(ml_dtypes.bfloat16)


def _deint_qkv(wq):
    w = np.array(wq, copy=True)
    deint = np.concatenate([np.arange(0, D, 2), np.arange(1, D, 2)])
    for s in range(2):
        for h in range(H):
            base = s * C + h * D
            w[base:base + D, :] = w[base + deint, :]
    return w


_BUILT = {}


def _get_program(upto="D"):
    if upto not in _BUILT:
        nc, es, pl = build_program(upto)
        _BUILT[upto] = finish_build(nc, es, pl)
    return _BUILT[upto]


def kernel(L0_attn_x, L0_attn_W_qkv, L0_attn_W_o, L0_ffn_w1, L0_ffn_w2,
           L0_ffn_w3, _upto="D"):
    xf = np.asarray(L0_attn_x, np.float32)
    wq = _deint_qkv(np.asarray(L0_attn_W_qkv, np.float32))
    wov = np.ascontiguousarray(np.asarray(L0_attn_W_o, np.float32))
    w1v = np.ascontiguousarray(np.asarray(L0_ffn_w1, np.float32))
    w2v = np.ascontiguousarray(np.asarray(L0_ffn_w2, np.float32))
    w3v = np.ascontiguousarray(np.asarray(L0_ffn_w3, np.float32))
    ident = np.eye(128, dtype=ml_dtypes.bfloat16)
    sin_de, cos_de = _host_tables()

    in_maps = []
    metas = []
    for core in range(8):
        b, p = core // 2, core % 2
        tiles, idx = _perm_for_parity(p)
        xc = np.ascontiguousarray(xf[b][idx])
        sc = np.ascontiguousarray(
            np.stack([sin_de[idx], cos_de[idx]]).astype(np.float32))
        in_maps.append({
            "x": xc, "wqkv": wq, "wo": wov, "w1": w1v, "w2": w2v, "w3": w3v,
            "sincos": sc, "masks": _masks_for_parity(p), "ident": ident,
        })
        metas.append((b, tiles))

    nc = _get_program(_upto)
    res = run_bass_kernel_spmd(nc, in_maps, core_ids=list(range(8)))

    if _upto != "D":
        return [res.results[c]["out"] for c in range(8)]

    outp = np.empty((B, T, C), np.float32)
    for core in range(8):
        b, tiles = metas[core]
        o = res.results[core]["out"]
        for j in range(OWN):
            g = tiles[j]
            outp[b, g * 128:(g + 1) * 128, :] = o[j * 128:(j + 1) * 128, :]
    return outp


# version 12
# speedup vs baseline: 1.2386x; 1.0202x over previous
import math
import numpy as np
import ml_dtypes
from contextlib import ExitStack

import concourse.bass as bass
import concourse.mybir as mybir
from concourse.bass_utils import run_bass_kernel_spmd

F32 = mybir.dt.float32
BF16 = mybir.dt.bfloat16
AF = mybir.ActivationFunctionType
ALU = mybir.AluOpType
AX = mybir.AxisListType

B, T, C, H, D, FF = 4, 4096, 2048, 16, 128, 5632
NT = 32
OWN = 16
NCC = C // 128
NHC = FF // 128
SCALE = 1.0 / math.sqrt(D)
LN_EPS = 1e-5
RMS_EPS = 1.1920929e-07
QEPS = 1e-8

ENGINES = ("sync", "scalar", "tensor", "vector", "gpsimd")


class Pl:

    def __init__(self, sems):
        self.q = {e: [] for e in ENGINES}
        self.sems = sems
        self.cnt = {k: 0 for k in sems}
        self.last_wait = {e: {} for e in ENGINES}

    def op(self, eng, fn, waits=(), inc=None, amt=1):
        eff = []
        for wv in waits:
            if wv is None:
                continue
            name, val = wv
            if val is None or val <= 0:
                continue
            if self.last_wait[eng].get(name, -1) >= val:
                continue
            self.last_wait[eng][name] = val
            eff.append((self.sems[name], val))
        if inc is not None:
            self.cnt[inc] += amt
        h = self.sems[inc] if inc is not None else None

        def g(e, _w=tuple(eff), _fn=fn, _h=h, _amt=amt):
            for (s, v) in _w:
                e.wait_ge(s, v)
            r = _fn(e)
            if _h is not None:
                r.then_inc(_h, _amt)
        self.q[eng].append(g)
        return self.cnt[inc] if inc is not None else None


def build_program(upto="D"):
    nc = bass.Bass("TRN2", target_bir_lowering=False)

    x = nc.dram_tensor("x", [T, C], F32, kind="ExternalInput").ap()
    wqkv = nc.dram_tensor("wqkv", [3 * C, C], F32, kind="ExternalInput").ap()
    wo = nc.dram_tensor("wo", [C, C], F32, kind="ExternalInput").ap()
    w1 = nc.dram_tensor("w1", [FF, C], F32, kind="ExternalInput").ap()
    w2 = nc.dram_tensor("w2", [FF, C], F32, kind="ExternalInput").ap()
    w3 = nc.dram_tensor("w3", [C, FF], F32, kind="ExternalInput").ap()
    sincos = nc.dram_tensor("sincos", [2, T, D], F32, kind="ExternalInput").ap()
    masks = nc.dram_tensor("masks", [8, 128, 512], BF16, kind="ExternalInput").ap()
    ident = nc.dram_tensor("ident", [128, 128], BF16, kind="ExternalInput").ap()
    out = nc.dram_tensor("out", [OWN * 128, C], F32, kind="ExternalOutput").ap()

    wqkv_bf = nc.dram_tensor("wqkv_bf", [3 * C, C], BF16, kind="Internal").ap()
    wo_bf = nc.dram_tensor("wo_bf", [C, C], BF16, kind="Internal").ap()
    qkvscr = nc.dram_tensor("qkvscr", [12, NT, 128, 512], BF16, kind="Internal").ap()
    w1qt = nc.dram_tensor("w1qt", [NHC, 128, 2048], BF16, kind="Internal").ap()
    w2qt = nc.dram_tensor("w2qt", [NHC, 128, 2048], BF16, kind="Internal").ap()
    w3q = nc.dram_tensor("w3q", [C, FF], BF16, kind="Internal").ap()
    sscr = nc.dram_tensor("sscr", [OWN, 128, 2048], BF16, kind="Internal").ap()
    w3qt = nc.dram_tensor("w3qt", [NHC, 128, 2048], BF16, kind="Internal").ap()
    woqt = nc.dram_tensor("woqt", [NCC, 128, 2048], BF16, kind="Internal").ap()

    es = ExitStack()
    E = es.enter_context
    xnT_own = E(nc.sbuf_tensor("xnT_own", [128, OWN * 2048], BF16))
    pool16 = E(nc.sbuf_tensor("pool16", [128, 16 * 2048], BF16))
    util = E(nc.sbuf_tensor("util", [128, 24576], BF16))
    masks_sb = E(nc.sbuf_tensor("masks_sb", [128, 4096], BF16))
    scs = E(nc.sbuf_tensor("scs", [128, 512], F32))
    attnT = E(nc.sbuf_tensor("attnT", [128, 1536], BF16))
    amsk = E(nc.sbuf_tensor("amsk", [128, 1536], BF16))
    avf = E(nc.sbuf_tensor("avf", [128, 512], F32))
    rden = E(nc.sbuf_tensor("rden", [1, 512], F32))
    ones1 = E(nc.sbuf_tensor("ones1", [1, 128], F32))
    ones128 = E(nc.sbuf_tensor("ones128", [128, 1], F32))
    onesb = E(nc.sbuf_tensor("onesb", [128, 1], BF16))
    ident_sb = E(nc.sbuf_tensor("ident_sb", [128, 128], BF16))
    lnv = E(nc.sbuf_tensor("lnv", [128, 16], F32))
    rmsv = E(nc.sbuf_tensor("rmsv", [128, 24], F32))
    statc = E(nc.sbuf_tensor("statc", [128, 48], F32))
    stat1 = E(nc.sbuf_tensor("stat1", [128, 2], F32))
    tau2 = E(nc.sbuf_tensor("tau2", [1, 2], F32))
    thrw = E(nc.sbuf_tensor("thrw", [128, 6], F32))
    epsv = E(nc.sbuf_tensor("epsv", [128, 4], F32))

    utilf = util.ap().bitcast(F32)
    ostg = utilf[:, 9216:10240]
    sgst = utilf[:, 10240:10752]
    gstg = utilf[:, 10752:11264]
    sigf = utilf[:, 11264:12288]

    def U(c0, w):
        return util[:, c0:c0 + w]

    def UF(c0, w):
        return utilf[:, c0:c0 + w]

    P = [E(nc.psum_tensor("P%d" % i, [128, 512], F32)) for i in range(8)]
    Pb = [p.ap().bitcast(BF16) for p in P]

    sems = {}
    for name in ("sLD", "sSC", "sGP", "sOUT", "sPE", "sACT", "sDVE",
                 "dp0", "dp1", "dp2", "dp3", "dp4", "dp5", "dp6", "dp7",
                 "sT0", "sT1", "sP0", "sP1", "sO0", "sO1", "sMK"):
        sems[name] = E(nc.semaphore(name))
    pl = Pl(sems)

    def dma(eng, out_ap, in_ap, waits=(), inc=None, transpose=False):
        def f(e, o=out_ap, i=in_ap, t=transpose):
            return e.dma_start(out=o, in_=i, transpose=True) if t \
                else e.dma_start(out=o, in_=i)
        return pl.op(eng, f, waits=waits, inc=inc, amt=16)

    def t_mm(out_ap, lhsT, rhs, start, stop, waits=(), inc=None):
        def f(e, o=out_ap, lt=lhsT, r=rhs, s1=start, s2=stop):
            return e.matmul(o, lt, r, start=s1, stop=s2)
        return pl.op("tensor", f, waits=waits, inc=inc)

    def t_tr(out_ap, in_ap, waits=(), inc=None):
        def f(e, o=out_ap, i=in_ap):
            return e.transpose(o, i, ident_sb[:])
        return pl.op("tensor", f, waits=waits, inc=inc)

    def a_act(out_ap, in_ap, func, scale=1.0, bias=0.0, accum=None, waits=()):
        def f(e, o=out_ap, i=in_ap, fu=func, s=scale, b=bias, a=accum):
            kw = {}
            if a is not None:
                kw["accum_out"] = a
            return e.activation(out=o, in_=i, func=fu, scale=s, bias=b, **kw)
        return pl.op("scalar", f, waits=waits, inc="sACT")

    def a_copy(out_ap, in_ap, waits=()):
        def f(e, o=out_ap, i=in_ap):
            return e.activation(out=o, in_=i, func=AF.Copy)
        return pl.op("scalar", f, waits=waits, inc="sACT")

    def v_red(out_ap, in_ap, op=ALU.add, absv=None, waits=()):
        def f(e, o=out_ap, i=in_ap, _op=op, _a=absv):
            return e.tensor_reduce(out=o, in_=i, axis=AX.X, op=_op,
                                   apply_absolute_value=_a)
        return pl.op("vector", f, waits=waits, inc="sDVE")

    def v_tt(out_ap, a_ap, b_ap, op=ALU.mult, waits=()):
        def f(e, o=out_ap, a=a_ap, b=b_ap, _op=op):
            return e.tensor_tensor(out=o, in0=a, in1=b, op=_op)
        return pl.op("vector", f, waits=waits, inc="sDVE")

    def v_stt(out_ap, in0, scalar, in1, op0, op1, accum=None, waits=()):
        def f(e, o=out_ap, a=in0, s=scalar, b=in1, o0=op0, o1=op1, ac=accum):
            kw = {}
            if ac is not None:
                kw["accum_out"] = ac
            return e.scalar_tensor_tensor(out=o, in0=a, scalar=s, in1=b,
                                          op0=o0, op1=o1, **kw)
        return pl.op("vector", f, waits=waits, inc="sDVE")

    def v_ts(out_ap, in_ap, s1, op0, waits=()):
        def f(e, o=out_ap, i=in_ap, _s=s1, _o0=op0):
            return e.tensor_scalar(out=o, in0=i, scalar1=_s, scalar2=None,
                                   op0=_o0)
        return pl.op("vector", f, waits=waits, inc="sDVE")

    def v_tsmul(out_ap, in_ap, c, waits=()):
        def f(e, o=out_ap, i=in_ap, _c=c):
            return e.tensor_scalar_mul(o, i, _c)
        return pl.op("vector", f, waits=waits, inc="sDVE")

    def v_copy(out_ap, in_ap, waits=()):
        def f(e, o=out_ap, i=in_ap):
            return e.tensor_copy(o, i)
        return pl.op("vector", f, waits=waits, inc="sDVE")

    def v_recip(out_ap, in_ap, waits=()):
        def f(e, o=out_ap, i=in_ap):
            return e.reciprocal(o, i)
        return pl.op("vector", f, waits=waits, inc="sDVE")

    def v_memset(ap_, val, waits=()):
        def f(e, o=ap_, v=val):
            return e.memset(o, v)
        return pl.op("vector", f, waits=waits, inc="sDVE")

    def nopwait(eng, waits):
        def f(e):
            return e.nop()
        return pl.op(eng, f, waits=waits)

    def gp_marker(waits):
        def f(e):
            return e.nop()
        return pl.op("gpsimd", f, waits=waits, inc="sMK")

    def own_xnT(cc, col, w=128):
        return xnT_own[:, cc * 2048 + col: cc * 2048 + col + w]

    def oth_xnT(cc, col, w=128):
        return pool16[:, cc * 2048 + col: cc * 2048 + col + w]

    def xnT_tile(cc, l):
        return own_xnT(cc, l * 128) if l < OWN else oth_xnT(cc, (l - OWN) * 128)

    def avT(h, col, w):
        return pool16[:, h * 2048 + col: h * 2048 + col + w]

    def hid(hc, col, w):
        return pool16[:, hc * 512 + col: hc * 512 + col + w]

    for mi in range(8):
        dma("sync", masks_sb[:, mi * 512:(mi + 1) * 512], masks[mi],
            inc="sLD")
    dma("sync", ident_sb[:], ident, inc="sLD")
    consts_ld = pl.cnt["sLD"]
    v_memset(ones1[:], 1.0)
    v_memset(ones128[:], 1.0)
    v_memset(onesb[:], 1.0)
    v_memset(epsv[:, 0:1], LN_EPS)
    v_memset(epsv[:, 1:2], RMS_EPS)
    v_memset(epsv[:, 2:3], 0.5 * QEPS)
    consts_dve = pl.cnt["sDVE"]

    qkv_cast = []
    for i in range(6):
        qkv_cast.append(dma("gpsimd", wqkv_bf[i * 1024:(i + 1) * 1024, :],
                            wqkv[i * 1024:(i + 1) * 1024, :], inc="sGP"))
    wo_cast = []
    for i in range(2):
        wo_cast.append(dma("gpsimd", wo_bf[i * 1024:(i + 1) * 1024, :],
                           wo[i * 1024:(i + 1) * 1024, :], inc="sGP"))

    a0_apply = [None] * NT
    a0_var = [None] * NT
    a0_evac = [[None] * NCC for _ in range(NT)]
    for l in range(NT):
        par = l % 2
        xb = UF(0 + par * 2048, 2048)
        sq = UF(4096, 2048)
        xnbf = U(12288 + par * 2048, 2048)
        lv = lnv[:, par * 8:par * 8 + 8]
        w = []
        if l >= 2:
            w.append(("sACT", a0_apply[l - 2]))
        xsem = "dp%d" % (l % 2)
        ldx = dma("sync", xb, x[l * 128:(l + 1) * 128, :], waits=w, inc=xsem)
        w = [(xsem, ldx)]
        if l >= 2:
            w.append(("sACT", a0_apply[l - 2]))
        r1 = v_red(lv[:, 0:1], xb, waits=w)
        w = [(xsem, ldx)]
        if l >= 2:
            w.append(("sDVE", a0_var[l - 2]))
        sq_a = a_act(sq, xb, AF.Square, accum=lv[:, 1:2], waits=w)
        r2 = v_tsmul(lv[:, 2:3], lv[:, 0:1], 1.0 / C, waits=[("sDVE", r1)])
        r3 = v_tt(lv[:, 3:4], lv[:, 2:3], lv[:, 2:3], ALU.mult,
                  waits=[("sDVE", r2)])
        var = v_stt(lv[:, 4:5], lv[:, 1:2], 1.0 / C, lv[:, 3:4],
                    ALU.mult, ALU.subtract,
                    waits=[("sACT", sq_a), ("sDVE", r3)])
        a0_var[l] = var
        sqv = a_act(lv[:, 5:6], lv[:, 4:5], AF.Sqrt, bias=epsv[:, 0:1],
                    waits=[("sDVE", var), ("sDVE", consts_dve)])
        rc0 = v_recip(lv[:, 6:7], lv[:, 5:6], waits=[("sACT", sqv)])
        nmr = v_stt(lv[:, 7:8], lv[:, 2:3], -1.0, lv[:, 6:7], ALU.mult,
                    ALU.mult, waits=[("sDVE", rc0)])
        w = [("sDVE", nmr)]
        if l >= 2:
            w.append(a0_evac[l - 2][NCC - 1])
        ap_a = a_act(xnbf, xb, AF.Identity, scale=lv[:, 6:7], bias=lv[:, 7:8],
                     waits=w)
        a0_apply[l] = ap_a
        dcol = l * 128 if l < OWN else (l - OWN) * 128
        dest = own_xnT if l < OWN else oth_xnT
        for cc in range(NCC):
            bank = 6 + (cc % 2)
            w = [("sACT", ap_a), ("sLD", consts_ld)]
            if cc >= 2:
                w.append(a0_evac[l][cc - 2])
            elif l >= 1:
                w.append(a0_evac[l - 1][NCC - 2 + cc])
            pe_v = t_tr(Pb[bank][:, 0:128], xnbf[:, cc * 128:(cc + 1) * 128],
                        waits=w, inc="sPE")
            if cc % 2 == 0:
                cnt = v_copy(dest(cc, dcol), Pb[bank][:, 0:128],
                             waits=[("sPE", pe_v)])
                a0_evac[l][cc] = ("sDVE", cnt)
            else:
                cnt = a_copy(dest(cc, dcol), Pb[bank][:, 0:128],
                             waits=[("sPE", pe_v)])
                a0_evac[l][cc] = ("sACT", cnt)
    a0_done = [("sACT", pl.cnt["sACT"]), ("sDVE", pl.cnt["sDVE"])]

    if upto == "A0":
        wv = list(a0_done)
        for cc in range(16):
            dma("gpsimd", out[cc * 128:(cc + 1) * 128, 0:2048],
                own_xnT(cc, 0, 2048), waits=wv, inc="sOUT")
        nopwait("sync", [("sOUT", pl.cnt["sOUT"])])
        return nc, es, pl

    wT = U(0, 8192)

    def wT_tile(cc):
        return wT[:, cc * 512:(cc + 1) * 512]

    q2p = [UF(7680, 512), UF(9728, 512)]
    tmp2p = [UF(8192, 512), UF(10240, 512)]
    sqsp = [UF(8704, 512), UF(10752, 512)]
    q3p = [UF(9216, 512), UF(11264, 512)]
    a1_mm_done = None
    a1_bank_free = [None, None]
    a1_stage_free = [None, None]
    a1_scs_free = [None, None]
    a1_q2_free = [None, None]
    cast_total = pl.cnt["sGP"]
    for osl in range(12):
        wv = [("sGP", cast_total)] + a0_done
        if a1_mm_done is not None:
            wv.append(("sPE", a1_mm_done))
        wld = None
        for cc in range(NCC):
            src = wqkv_bf[osl * 512:(osl + 1) * 512, cc * 128:(cc + 1) * 128]
            wld = dma("scalar", wT_tile(cc), src, waits=wv if cc == 0 else (),
                      inc="sSC", transpose=True)
        lrange = range(OWN) if osl < 4 else range(NT)
        for l in lrange:
            par = l % 2
            bank = par
            is_v = osl >= 8
            scl = None
            ssem = "dp%d" % (2 + par)
            if not is_v:
                wv = []
                if a1_scs_free[par] is not None:
                    wv.append(("sDVE", a1_scs_free[par]))
                dma("sync", scs[:, par * 256:par * 256 + 128],
                    sincos[0, l * 128:(l + 1) * 128, :], waits=wv, inc=ssem)
                scl = dma("sync", scs[:, par * 256 + 128:par * 256 + 256],
                          sincos[1, l * 128:(l + 1) * 128, :], inc=ssem)
            wv = [("sSC", wld)] + a0_done
            if a1_bank_free[par] is not None:
                wv.append(a1_bank_free[par])
            pe_v = None
            for cc in range(NCC):
                pe_v = t_mm(P[bank][:, 0:512], xnT_tile(cc, l), wT_tile(cc),
                            start=(cc == 0), stop=(cc == NCC - 1),
                            waits=wv if cc == 0 else (),
                            inc="sPE" if cc == NCC - 1 else None)
            a1_mm_done = pe_v
            st = U(23552 + par * 512, 512)
            if is_v:
                w2v = [("sPE", pe_v)]
                if a1_stage_free[par] is not None:
                    w2v.append(("sT%d" % par, a1_stage_free[par]))
                cp = a_copy(st, P[bank][:, 0:512], waits=w2v)
                a1_bank_free[par] = ("sACT", cp)
                a1_stage_free[par] = dma("gpsimd", qkvscr[osl, l], st,
                                         waits=[("sACT", cp)],
                                         inc="sT%d" % par)
            else:
                q2 = q2p[par]
                tmp2 = tmp2p[par]
                sqs = sqsp[par]
                q3 = q3p[par]
                rv = rmsv[:, par * 12:par * 12 + 12]
                sinb = scs[:, par * 256:par * 256 + 64]
                cosb = scs[:, par * 256 + 128:par * 256 + 256]
                sin_b = sinb.unsqueeze(1).broadcast_to([128, 4, 64])
                cos_b = cosb.unsqueeze(1).broadcast_to([128, 4, 128])
                pk3 = P[bank][:, 0:512].rearrange("p (h z) -> p h z", h=4)
                q23 = q2.rearrange("p (h z) -> p h z", h=4)
                wv = [("sPE", pe_v), (ssem, scl)]
                if a1_q2_free[par] is not None:
                    wv.append(("sACT", a1_q2_free[par]))
                v_stt(q23[:, :, 0:64], pk3[:, :, 64:128], -1.0, sin_b,
                      ALU.mult, ALU.mult, waits=wv)
                v_stt(q23[:, :, 64:128], pk3[:, :, 0:64], 1.0, sin_b,
                      ALU.mult, ALU.mult)
                cosm = v_tt(tmp2.rearrange("p (h z) -> p h z", h=4),
                            pk3, cos_b, ALU.mult)
                a1_bank_free[par] = ("sDVE", cosm)
                a1_scs_free[par] = cosm
                ad = v_tt(q3, q2, tmp2, ALU.add, waits=[("sDVE", cosm)])
                for hh in range(4):
                    v_stt(sqs[:, hh * 128:(hh + 1) * 128],
                          q3[:, hh * 128:(hh + 1) * 128], 1.0,
                          q3[:, hh * 128:(hh + 1) * 128], ALU.mult, ALU.mult,
                          accum=rv[:, hh:hh + 1],
                          waits=[("sDVE", ad)] if hh == 0 else ())
                dv = pl.cnt["sDVE"]
                sr = a_act(rv[:, 4:8], rv[:, 0:4], AF.Sqrt, scale=1.0 / D,
                           bias=epsv[:, 1:2],
                           waits=[("sDVE", dv), ("sDVE", consts_dve)])
                rc = v_recip(rv[:, 8:12], rv[:, 4:8], waits=[("sACT", sr)])
                w2v = [("sDVE", rc)]
                if a1_stage_free[par] is not None:
                    w2v.append(("sT%d" % par, a1_stage_free[par]))
                cp = None
                for hh in range(4):
                    cp = a_act(st[:, hh * 128:(hh + 1) * 128],
                               q3[:, hh * 128:(hh + 1) * 128], AF.Identity,
                               scale=rv[:, 8 + hh:9 + hh],
                               waits=w2v if hh == 0 else ())
                a1_q2_free[par] = cp
                a1_stage_free[par] = dma("gpsimd", qkvscr[osl, l], st,
                                         waits=[("sACT", cp)],
                                         inc="sT%d" % par)
    a1_scr_done = [("sMK", gp_marker([("sT0", pl.cnt["sT0"]),
                                      ("sT1", pl.cnt["sT1"])]))]
    a1_done = [("sACT", pl.cnt["sACT"]), ("sDVE", pl.cnt["sDVE"]),
               ("sPE", pl.cnt["sPE"])]

    if upto == "A1":
        wv = list(a1_scr_done)
        for i, (osl, l) in enumerate([(0, 0), (0, 15), (5, 0), (5, 17),
                                      (9, 0), (9, 17), (3, 15), (7, 31)]):
            dma("gpsimd", out[i * 128:(i + 1) * 128, 0:512],
                qkvscr[osl, l], waits=wv, inc="sOUT")
        nopwait("sync", [("sOUT", pl.cnt["sOUT"])])
        return nc, es, pl

    wflat = {
        "w1": w1.rearrange("(r p) c -> r p c", p=128),
        "w2": w2.rearrange("(r p) c -> r p c", p=128),
        "w3": w3.flatten().rearrange("(r p c) -> r p c", p=128, c=2048),
    }
    NBLK = 44
    NW = float(FF * C)
    wf_free = [None, None]
    ta_prev = None
    cp3_prev = None
    for wi, wname in enumerate(("w1", "w2", "w3")):
        for r in range(NBLK):
            par = r % 2
            wf = UF(5120 + par * 2048, 2048)
            wv = ([wf_free[par]] if wf_free[par] is not None else []) + \
                (a1_done if (wi == 0 and r < 2) else [])
            ld = dma("sync", wf, wflat[wname][r], waits=wv,
                     inc="dp%d" % par)
            dv = v_red(statc[:, r:r + 1], wf, absv=True,
                       waits=[("dp%d" % par, ld)])
            wf_free[par] = ("sDVE", dv)
        dv = v_red(stat1[:, 0:1], statc[:, 0:NBLK], waits=[("sDVE", dv)])
        wv = [("sDVE", dv), ("sDVE", consts_dve)]
        if ta_prev is not None:
            wv.append(("sACT", ta_prev))
        pe_v = t_mm(P[2][0:1, 0:1], ones128[:], stat1[:, 0:1], True, True,
                    waits=wv, inc="sPE")
        ta = a_act(tau2[0:1, 0:1], P[2][0:1, 0:1], AF.Identity,
                   scale=0.5 / NW, bias=epsv[0:1, 2:3],
                   waits=[("sPE", pe_v), ("sDVE", consts_dve)])
        ta_prev = ta
        nt_ = v_tsmul(tau2[0:1, 1:2], tau2[0:1, 0:1], -1.0,
                      waits=[("sACT", ta)])
        wv = [("sDVE", nt_)]
        if cp3_prev is not None:
            wv.append(("sDVE", cp3_prev))
        pe_v = t_mm(P[3][:, 0:2], ones1[:], tau2[0:1, 0:2], True, True,
                    waits=wv, inc="sPE")
        cp3_prev = v_copy(thrw[:, wi * 2:wi * 2 + 2], P[3][:, 0:2],
                          waits=[("sPE", pe_v)])
    thr_dve = pl.cnt["sDVE"]
    stats_act = pl.cnt["sACT"]

    qt_free = [None, None]
    st_free = [None, None]
    last_qd = thr_dve
    wf_free = [("sDVE", thr_dve), ("sDVE", thr_dve)]
    cp_prev_hc = stats_act
    for wi, wsrc, wdst in ((0, w1, w1qt), (1, w2, w2qt)):
        for hc in range(NHC):
            par = hc % 2
            wf = UF(5120 + par * 2048, 2048)
            qt = U(0 + par * 2048, 2048)
            st = U(4096 + par * 2048, 2048)
            wv = [wf_free[par]] if wf_free[par] is not None else []
            ld = dma("sync", wf, wsrc[hc * 128:(hc + 1) * 128, :], waits=wv,
                     inc="dp%d" % par)
            wv = [("dp%d" % par, ld), ("sDVE", last_qd)] + \
                (a1_done if hc < 2 else [])
            if qt_free[par] is not None:
                wv.append(qt_free[par])
            v_ts(qt, wf, thrw[:, wi * 2:wi * 2 + 1], ALU.is_gt, waits=wv)
            lt = v_ts(U(8192, 2048), wf, thrw[:, wi * 2 + 1:wi * 2 + 2],
                      ALU.is_lt)
            qd = v_tt(qt, qt, U(8192, 2048), ALU.subtract,
                      waits=[("sDVE", lt)])
            wf_free[par] = ("sDVE", qd)
            last_qd = qd
            cp = cp_prev_hc
            for g4 in range(4):
                bank = 6 + (g4 % 2)
                wvt = [("sDVE", qd)]
                if cp is not None:
                    wvt.append(("sACT", cp))
                pe_v = None
                for k in range(4):
                    dc = g4 * 4 + k
                    pe_v = t_tr(Pb[bank][:, k * 128:(k + 1) * 128],
                                qt[:, dc * 128:(dc + 1) * 128],
                                waits=wvt if k == 0 else (),
                                inc="sPE" if k == 3 else None)
                wvc = [("sPE", pe_v)]
                if g4 == 0 and st_free[par] is not None:
                    wvc.append(("sT%d" % par, st_free[par]))
                cp = a_copy(st[:, g4 * 512:(g4 + 1) * 512], Pb[bank][:, 0:512],
                            waits=wvc)
            qt_free[par] = ("sACT", cp)
            cp_prev_hc = cp
            st_free[par] = dma("gpsimd", wdst[hc], st, waits=[("sACT", cp)],
                               inc="sT%d" % par)
    for r in range(16):
        for ci, cw in ((0, 2048), (2048, 2048), (4096, 1536)):
            par = (r * 3 + ci // 2048) % 2
            wf = UF(5120 + par * 2048, cw)
            qt = U(0 + par * 2048, cw)
            wv = [wf_free[par]] if wf_free[par] is not None else []
            ld = dma("sync", wf, w3[r * 128:(r + 1) * 128, ci:ci + cw],
                     waits=wv, inc="dp%d" % par)
            wv = [("dp%d" % par, ld), ("sDVE", last_qd)]
            if qt_free[par] is not None:
                wv.append(qt_free[par])
            v_ts(qt, wf, thrw[:, 4:5], ALU.is_gt, waits=wv)
            lt = v_ts(U(8192, cw), wf, thrw[:, 5:6], ALU.is_lt)
            qd = v_tt(qt, qt, U(8192, cw), ALU.subtract,
                      waits=[("sDVE", lt)])
            wf_free[par] = ("sDVE", qd)
            last_qd = qd
            gp = dma("gpsimd", w3q[r * 128:(r + 1) * 128, ci:ci + cw], qt,
                     waits=[("sDVE", qd)], inc="sT%d" % par)
            qt_free[par] = ("sT%d" % par, gp)
    quant_scr = [("sMK", gp_marker([("sT0", pl.cnt["sT0"]),
                                    ("sT1", pl.cnt["sT1"])]))]
    quant_act = pl.cnt["sACT"]
    quant_dve = pl.cnt["sDVE"]
    quant_pe = pl.cnt["sPE"]

    if upto == "QUANT":
        wv = list(quant_scr)
        dma("gpsimd", out[0:128, 0:2048], w1qt[3], waits=wv, inc="sOUT")
        dma("gpsimd", out[128:256, 0:2048], w2qt[40], waits=wv, inc="sOUT")
        dma("gpsimd", out[256:384, 0:2048], w3q[512:640, 1024:3072], waits=wv,
            inc="sOUT")
        nopwait("sync", [("sOUT", pl.cnt["sOUT"])])
        return nc, es, pl

    def b_bufs(hp):
        if hp == 0:
            return U(0, 4096), U(4096, 4096), U(8192, 2048)
        return U(10240, 4096), U(14336, 4096), U(18432, 2048)

    def b_issue_loads(h, wv):
        kTb, vbb, qTb = b_bufs(h % 2)
        os_q, os_k, os_v = h // 4, 4 + h // 4, 8 + h // 4
        dcol = (h % 4) * 128
        ksem = "dp%d" % (4 + h % 2)
        vsem = "dp%d" % (2 + h % 2)
        ksl = qkvscr[os_k][:, :, dcol:dcol + 128].rearrange("l t d -> (l t) d")
        kld = dma("scalar", kTb, ksl, waits=wv, inc=ksem, transpose=True)
        qld = dma("scalar", qTb, qkvscr[os_q][0:OWN][:, :, dcol:dcol + 128]
                  .rearrange("l t d -> (l t) d"), inc=ksem, transpose=True)
        vld = None
        for l in range(NT):
            vld = dma("sync", vbb[:, l * 128:(l + 1) * 128],
                      qkvscr[os_v, l][:, dcol:dcol + 128],
                      waits=wv if l == 0 else (), inc=vsem)
        return (ksem, qld), (vsem, vld)

    b_prev_pe = [None, None]
    exp_done = [None, None, None]
    mask_done = [None, None, None]
    avden_done = [None, None, None]
    SBK = (2, 3, 6)
    cp_prev = None
    nv_prev = None
    b_first = [("sACT", max(stats_act, quant_act)), ("sDVE", thr_dve),
               ("sDVE", quant_dve), ("sPE", quant_pe)] + \
        a1_scr_done + quant_scr
    pt_jobs = []
    pt_state = {"free": [None, None], "done": [0]}

    def mk_pt(src_ap, dst_ap, idx):
        def go():
            par = idx % 2
            tb_ = U(20480 + par * 2048, 2048)
            wv = list(quant_scr)
            if pt_state["free"][par] is not None:
                wv.append(("sP%d" % par, pt_state["free"][par]))
            tl = dma("scalar", tb_, src_ap, waits=wv, inc="dp%d" % (6 + par),
                     transpose=True)
            pt_state["free"][par] = dma("gpsimd", dst_ap, tb_,
                                        waits=[("dp%d" % (6 + par), tl)],
                                        inc="sP%d" % par)
        return go
    for hc in range(NHC):
        pt_jobs.append(mk_pt(w3q[:, hc * 128:(hc + 1) * 128], w3qt[hc], hc))
    for cc in range(NCC):
        pt_jobs.append(mk_pt(wo_bf[:, cc * 128:(cc + 1) * 128], woqt[cc],
                             NHC + cc))
    pt_per_head = (len(pt_jobs) + H - 1) // H

    b_loads = [None] * (H + 1)
    b_loads[0] = b_issue_loads(0, list(b_first))
    for h in range(H):
        kTb, vbb, qTb = b_bufs(h % 2)
        qld, vld = b_loads[h]
        if h + 1 < H:
            wv = list(a1_scr_done)
            if b_prev_pe[(h + 1) % 2] is not None:
                wv.append(("sPE", b_prev_pe[(h + 1) % 2]))
            b_loads[h + 1] = b_issue_loads(h + 1, wv)
        for _ in range(pt_per_head):
            if pt_jobs:
                pt_jobs.pop(0)()
        for g in range(4):
            qs = qTb[:, g * 512:(g + 1) * 512]
            chunks = [(0, i) for i in range(4 * g + 4)] + \
                     [(1, i) for i in range(4 * g + 4)]
            n_ch = len(chunks)
            av_pe = None

            def chunk_mask(half, idx):
                if half == 0 and idx >= 4 * g:
                    return idx - 4 * g
                if half == 1 and idx >= 4 * g:
                    return 4 + (idx - 4 * g)
                return None

            sc_pe_l = [None] * n_ch

            def emit_scores(ci):
                half, idx = chunks[ci]
                par = ci % 3
                bank = SBK[par]
                kcol = idx * 128 if half == 0 else 2048 + idx * 128
                wv = [qld, vld] + b_first
                if exp_done[par] is not None:
                    wv.append(("sACT", exp_done[par]))
                sc_pe_l[ci] = t_mm(P[bank][:, 0:512], kTb[:, kcol:kcol + 128],
                                   qs, True, True, waits=wv, inc="sPE")

            def emit_exp(ci):
                half, idx = chunks[ci]
                par = ci % 3
                bank = SBK[par]
                wv = [("sPE", sc_pe_l[ci])]
                if avden_done[par] is not None:
                    wv.append(("sPE", avden_done[par]))
                ex = a_act(attnT[:, par * 512:par * 512 + 512],
                           P[bank][:, 0:512], AF.Exp, scale=SCALE, waits=wv)
                exp_done[par] = ex
                mid = chunk_mask(half, idx)
                if mid is not None:
                    wv = [("sACT", ex), ("sLD", consts_ld)]
                    if avden_done[par] is not None:
                        wv.append(("sPE", avden_done[par]))
                    mk = v_tt(amsk[:, par * 512:par * 512 + 512],
                              attnT[:, par * 512:par * 512 + 512],
                              masks_sb[:, mid * 512:(mid + 1) * 512],
                              ALU.mult, waits=wv)
                    mask_done[par] = mk
                    return ("sDVE", mk), amsk[:, par * 512:par * 512 + 512]
                return ("sACT", ex), attnT[:, par * 512:par * 512 + 512]

            def emit_avden(ci, gate, src):
                nonlocal av_pe, cp_prev
                half, idx = chunks[ci]
                par = ci % 3
                kcol = idx * 128 if half == 0 else 2048 + idx * 128
                first = (ci == 0)
                last = (ci == n_ch - 1)
                wv = [gate]
                if first and cp_prev is not None:
                    wv.append(("sACT", cp_prev))
                t_mm(P[4][:, 0:512], vbb[:, kcol:kcol + 128], src, first,
                     last, waits=wv)
                av_pe = t_mm(P[5][0:1, 0:512], onesb[:], src, first, last,
                             inc="sPE")
                avden_done[par] = av_pe

            gates = [None] * n_ch
            emit_scores(0)
            if n_ch > 1:
                emit_scores(1)
            for ci in range(n_ch):
                if ci + 2 < n_ch:
                    emit_scores(ci + 2)
                gates[ci] = emit_exp(ci)
                emit_avden(ci, *gates[ci])
            rc = v_recip(rden[0:1, 0:512], P[5][0:1, 0:512],
                         waits=[("sPE", av_pe), ("sDVE", consts_dve)])
            bc_pe = t_mm(P[0][:, 0:512], ones1[:], rden[0:1, 0:512], True,
                         True, waits=[("sDVE", rc)], inc="sPE")
            wv = [("sPE", av_pe)]
            if nv_prev is not None:
                wv.append(("sDVE", nv_prev))
            cp_prev = a_copy(avf[:], P[4][:, 0:512], waits=wv)
            nv_prev = v_tt(avT(h, g * 512, 512), avf[:], P[0][:, 0:512],
                           ALU.mult, waits=[("sPE", bc_pe),
                                            ("sACT", cp_prev)])
        b_prev_pe[h % 2] = bc_pe
    while pt_jobs:
        pt_jobs.pop(0)()
    pt_done = [("sMK", gp_marker([("sP0", pl.cnt["sP0"]),
                                  ("sP1", pl.cnt["sP1"])]))]
    b_done = [("sDVE", pl.cnt["sDVE"]), ("sPE", pl.cnt["sPE"]),
              ("sACT", pl.cnt["sACT"])]

    if upto == "B":
        wv = list(b_done)
        for h in range(16):
            dma("gpsimd", out[h * 128:(h + 1) * 128, 0:2048],
                avT(h, 0, 2048), waits=wv, inc="sOUT")
        nopwait("sync", [("sOUT", pl.cnt["sOUT"])])
        return nc, es, pl

    c_wo_free = [None, None]
    c_stage_free = [None, None]
    c_evac_act = None
    c_lds = {}

    def c_load(k):
        jp, cc = divmod(k, NCC)
        par = cc % 2
        wob = U(0 + par * 2048, 2048)
        wv = list(pt_done)
        if k < 2:
            wv += b_done
        if c_wo_free[par] is not None:
            wv.append(("sPE", c_wo_free[par]))
        c_lds[k] = dma("scalar", wob, woqt[cc], waits=wv, inc="dp%d" % par)

    c_load(0)
    for jp in range(8):
        pe_cc = None
        for cc in range(NCC):
            par = cc % 2
            wob = U(0 + par * 2048, 2048)
            k = jp * NCC + cc
            if k + 1 < 8 * NCC:
                c_load(k + 1)
            wv = [("dp%d" % par, c_lds[k])] + (b_done if (jp == 0 and
                                                          cc == 0) else [])
            if cc == 0 and c_evac_act is not None:
                wv.append(("sACT", c_evac_act))
            for sj in range(2):
                jt = 2 * jp + sj
                for oc in range(4):
                    bank = sj * 4 + oc
                    pe_cc = t_mm(P[bank][:, 0:512], avT(cc, jt * 128, 128),
                                 wob[:, oc * 512:(oc + 1) * 512],
                                 start=(cc == 0), stop=(cc == NCC - 1),
                                 waits=wv if (sj == 0 and oc == 0) else (),
                                 inc="sPE" if (sj == 1 and oc == 3) else None)
            c_wo_free[par] = pe_cc
        for sj in range(2):
            jt = 2 * jp + sj
            st = U(4096 + sj * 2048, 2048)
            cp = None
            for oc in range(4):
                bank = sj * 4 + oc
                wv = [("sPE", pe_cc)]
                if oc == 0 and c_stage_free[sj] is not None:
                    wv.append(("sT%d" % sj, c_stage_free[sj]))
                cp = a_copy(st[:, oc * 512:(oc + 1) * 512], P[bank][:, 0:512],
                            waits=wv)
            c_stage_free[sj] = dma("gpsimd", sscr[jt], st,
                                   waits=[("sACT", cp)], inc="sT%d" % sj)
        c_evac_act = pl.cnt["sACT"]
    c_scr_done = [("sMK", gp_marker([("sT0", pl.cnt["sT0"]),
                                     ("sT1", pl.cnt["sT1"])]))]
    c_done_act = pl.cnt["sACT"]
    c_done_pe = pl.cnt["sPE"]

    if upto == "C":
        wv = list(c_scr_done)
        for jt in range(16):
            dma("gpsimd", out[jt * 128:(jt + 1) * 128, 0:2048], sscr[jt],
                waits=wv, inc="sOUT")
        nopwait("sync", [("sOUT", pl.cnt["sOUT"])])
        return nc, es, pl

    d_w_free = [None, None]
    d_w3_free = [None, None]
    d_hid_free = None
    d_sig_free = [None, None]
    d_pg_act = c_done_act
    d_stile_free = None
    d_m1_prev = None
    o1_prev = None
    d_ostg_free = [None, None]
    d12 = {}

    def d12_load(tb, hc):
        par = hc % 2
        w1b = U(0 + par * 2048, 2048)
        w2b = U(4096 + par * 2048, 2048)
        wv = list(quant_scr)
        if tb == 0 and hc < 2:
            wv += [("sPE", c_done_pe)] + c_scr_done
        if d_w_free[par] is not None:
            wv.append(("sPE", d_w_free[par]))
        dma("scalar", w1b, w1qt[hc], waits=wv, inc="dp%d" % (2 + par))
        d12[hc] = dma("scalar", w2b, w2qt[hc], inc="dp%d" % (2 + par))

    for tb in range(4):
        px_free = [None, None, None, None]
        d12_load(tb, 0)
        for hc in range(NHC):
            par = hc % 2
            w1b = U(0 + par * 2048, 2048)
            w2b = U(4096 + par * 2048, 2048)
            if hc + 1 < NHC:
                d12_load(tb, hc + 1)
            l2 = d12[hc]
            wv = [("dp%d" % (2 + par), l2), ("sACT", d_pg_act)]
            if px_free[par] is not None:
                wv.append(px_free[par])
            if px_free[2 + par] is not None:
                wv.append(px_free[2 + par])
            for dc in range(NCC):
                t_mm(P[par][:, 0:512], w1b[:, dc * 128:(dc + 1) * 128],
                     own_xnT(dc, tb * 512, 512), start=(dc == 0),
                     stop=(dc == NCC - 1), waits=wv if dc == 0 else ())
            pe_v = None
            for dc in range(NCC):
                pe_v = t_mm(P[2 + par][:, 0:512],
                            w2b[:, dc * 128:(dc + 1) * 128],
                            own_xnT(dc, tb * 512, 512), start=(dc == 0),
                            stop=(dc == NCC - 1),
                            inc="sPE" if dc == NCC - 1 else None)
            d_w_free[par] = pe_v
            wv = [("sPE", pe_v)]
            if d_sig_free[par] is not None:
                wv.append(("sDVE", d_sig_free[par]))
            sg_a = a_act(sigf[:, par * 512:par * 512 + 512], P[par][:, 0:512],
                         AF.Sigmoid, waits=wv)
            wv = [("sACT", sg_a)]
            if d_hid_free is not None:
                wv.append(("sPE", d_hid_free))
            hv = v_tt(hid(hc, 0, 512), P[2 + par][:, 0:512],
                      sigf[:, par * 512:par * 512 + 512], ALU.mult, waits=wv)
            d_sig_free[par] = hv
            px_free[par] = ("sACT", sg_a)
            px_free[2 + par] = ("sDVE", hv)
        hid_dve = pl.cnt["sDVE"]
        if upto == "D1" and tb == 0:
            wv = [("sDVE", hid_dve)]
            for hcx in range(16):
                dma("gpsimd", out[hcx * 128:(hcx + 1) * 128, 0:512],
                    hid(hcx, 0, 512), waits=wv, inc="sOUT")
            nopwait("sync", [("sOUT", pl.cnt["sOUT"])])
            return nc, es, pl
        if upto == "D2" and tb == 0:
            w3b = U(8192, 2048)
            l3 = dma("scalar", w3b, w3q[:, 0:128], waits=list(quant_scr),
                     inc="sSC", transpose=True)
            pl.op("gpsimd", lambda e: e.nop(), waits=[("sSC", l3)])
            dma("gpsimd", out[0:128, 0:2048], w3b, inc="sOUT")
            dma("gpsimd", out[128:256, 0:2048], w3q[1536:1664, 0:2048],
                inc="sOUT")
            dma("gpsimd", out[256:384, 0:2048], w3q[1920:2048, 3584:5632]
                if False else w3q[1920:2048, 0:2048], inc="sOUT")
            nopwait("sync", [("sOUT", pl.cnt["sOUT"])])
            return nc, es, pl
        for tsp in range(2):
            mm_done = None
            d3 = {}

            def d3_load(hc):
                par = hc % 2
                w3b = U(8192 + par * 2048, 2048)
                wv = list(pt_done)
                if tb == 0 and tsp == 0 and hc < 2:
                    wv.append(("sPE", c_done_pe))
                if d_w3_free[par] is not None:
                    wv.append(("sPE", d_w3_free[par]))
                d3[hc] = dma("scalar", w3b, w3qt[hc], waits=wv,
                             inc="dp%d" % (4 + par))

            d3_load(0)
            for hc in range(NHC):
                par = hc % 2
                w3b = U(8192 + par * 2048, 2048)
                if hc + 1 < NHC:
                    d3_load(hc + 1)
                l3 = d3[hc]
                wv = [("dp%d" % (4 + par), l3), ("sDVE", hid_dve),
                      ("sACT", d_pg_act)]
                pe_v = None
                for ts in range(2):
                    for oc in range(4):
                        bank = ts * 4 + oc
                        pe_v = t_mm(P[bank][:, 0:512],
                                    hid(hc, (tsp * 2 + ts) * 128, 128),
                                    w3b[:, oc * 512:(oc + 1) * 512],
                                    start=(hc == 0), stop=(hc == NHC - 1),
                                    waits=wv if (ts == 0 and oc == 0) else (),
                                    inc="sPE" if (ts == 1 and oc == 3)
                                    else None)
                d_w3_free[par] = pe_v
            mm_done = pe_v
            if upto == "D3" and tb == 0 and tsp == 0:
                for k, bk in enumerate((2, 3, 6, 7)):
                    cpd = a_copy(ostg[:, 0:512], P[bk][:, 0:512],
                                 waits=[("sPE", mm_done)])
                    dg = dma("gpsimd", out[k * 128:(k + 1) * 128, 0:512],
                             ostg[:, 0:512], waits=[("sACT", cpd)], inc="sOUT")
                    pl.op("scalar", lambda e: e.nop(),
                          waits=[("sOUT", pl.cnt["sOUT"])])
                nopwait("sync", [("sOUT", pl.cnt["sOUT"])])
                return nc, es, pl
            for ts in range(2):
                jt = tb * 4 + tsp * 2 + ts
                stile = U(12288, 2048)
                xtile = UF(7168, 2048)
                wv = list(c_scr_done) + list(pt_done)
                if d_stile_free is not None:
                    wv.append(("sDVE", d_stile_free))
                sl = dma("sync", stile, sscr[jt], waits=wv, inc="dp6")
                xl = dma("sync", xtile, x[jt * 128:(jt + 1) * 128, :],
                         inc="dp7")
                m1 = d_m1_prev
                for oc in range(4):
                    bank = ts * 4 + oc
                    par = oc % 2
                    wv = [("sPE", mm_done)]
                    if m1 is not None:
                        wv.append(("sDVE", m1))
                    sv = a_act(gstg[:], P[bank][:, 0:512], AF.Silu, waits=wv)
                    wv = [("sACT", sv), ("dp6", sl)]
                    if o1_prev is not None:
                        wv.append(("sDVE", o1_prev))
                    m1 = v_tt(sgst[:], gstg[:],
                              stile[:, oc * 512:(oc + 1) * 512], ALU.mult,
                              waits=wv)
                    wv = [("dp7", xl), ("sDVE", m1)]
                    if d_ostg_free[par] is not None:
                        wv.append(("sO%d" % par, d_ostg_free[par]))
                    o1 = v_tt(ostg[:, par * 512:par * 512 + 512], sgst[:],
                              xtile[:, oc * 512:(oc + 1) * 512], ALU.add,
                              waits=wv)
                    o1_prev = o1
                    d_ostg_free[par] = dma(
                        "sync", out[jt * 128:(jt + 1) * 128,
                                    oc * 512:(oc + 1) * 512],
                        ostg[:, par * 512:par * 512 + 512],
                        waits=[("sDVE", o1)], inc="sO%d" % par)
                d_stile_free = m1
                d_m1_prev = m1
            d_pg_act = pl.cnt["sACT"]
            d_hid_free = mm_done
    nopwait("sync", [("sO0", pl.cnt["sO0"]), ("sO1", pl.cnt["sO1"])])
    return nc, es, pl


def finish_build(nc, es, pl):
    with es:
        with nc.Block() as block:
            @block.sync
            def _(e):
                for f in pl.q["sync"]:
                    f(e)

            @block.scalar
            def _(e):
                for f in pl.q["scalar"]:
                    f(e)

            @block.tensor
            def _(e):
                for f in pl.q["tensor"]:
                    f(e)

            @block.vector
            def _(e):
                for f in pl.q["vector"]:
                    f(e)

            @block.gpsimd
            def _(e):
                for f in pl.q["gpsimd"]:
                    f(e)
    return nc


def _host_tables():
    inv_freq = (1.0 / (10000.0 ** (np.arange(0, D, 2, dtype=np.float32) / D))
                ).astype(np.float32)
    t = np.arange(T, dtype=np.float32)
    freqs = t[:, None] * inv_freq[None, :]
    sin_b = np.sin(freqs).astype(np.float32)
    cos_b = np.cos(freqs).astype(np.float32)
    sin_de = np.concatenate([sin_b, sin_b], axis=1)
    cos_de = np.concatenate([cos_b, cos_b], axis=1)
    return sin_de, cos_de


def _perm_for_parity(p):
    tiles = list(range(p, 32, 2)) + list(range(1 - p, 32, 2))
    idx = np.concatenate([np.arange(g * 128, (g + 1) * 128) for g in tiles])
    return np.asarray(tiles), idx


def _masks_for_parity(p):
    tri = np.triu(np.ones((128, 128), np.float32))
    ones = np.ones((128, 128), np.float32)
    zeros = np.zeros((128, 128), np.float32)
    ms = []
    for j in range(4):
        blocks = [zeros if sct < j else (tri if sct == j else ones)
                  for sct in range(4)]
        ms.append(np.concatenate(blocks, axis=1))
    for j in range(4):
        if p == 0:
            blocks = [ones if sct > j else zeros for sct in range(4)]
        else:
            blocks = [ones if sct >= j else zeros for sct in range(4)]
        ms.append(np.concatenate(blocks, axis=1))
    return np.stack(ms).astype(ml_dtypes.bfloat16)


def _deint_qkv(wq):
    w = np.array(wq, copy=True)
    deint = np.concatenate([np.arange(0, D, 2), np.arange(1, D, 2)])
    for s in range(2):
        for h in range(H):
            base = s * C + h * D
            w[base:base + D, :] = w[base + deint, :]
    return w


_BUILT = {}


def _get_program(upto="D"):
    if upto not in _BUILT:
        nc, es, pl = build_program(upto)
        _BUILT[upto] = finish_build(nc, es, pl)
    return _BUILT[upto]


def kernel(L0_attn_x, L0_attn_W_qkv, L0_attn_W_o, L0_ffn_w1, L0_ffn_w2,
           L0_ffn_w3, _upto="D"):
    xf = np.asarray(L0_attn_x, np.float32)
    wq = _deint_qkv(np.asarray(L0_attn_W_qkv, np.float32))
    wov = np.ascontiguousarray(np.asarray(L0_attn_W_o, np.float32))
    w1v = np.ascontiguousarray(np.asarray(L0_ffn_w1, np.float32))
    w2v = np.ascontiguousarray(np.asarray(L0_ffn_w2, np.float32))
    w3v = np.ascontiguousarray(np.asarray(L0_ffn_w3, np.float32))
    ident = np.eye(128, dtype=ml_dtypes.bfloat16)
    sin_de, cos_de = _host_tables()

    in_maps = []
    metas = []
    for core in range(8):
        b, p = core // 2, core % 2
        tiles, idx = _perm_for_parity(p)
        xc = np.ascontiguousarray(xf[b][idx])
        sc = np.ascontiguousarray(
            np.stack([sin_de[idx], cos_de[idx]]).astype(np.float32))
        in_maps.append({
            "x": xc, "wqkv": wq, "wo": wov, "w1": w1v, "w2": w2v, "w3": w3v,
            "sincos": sc, "masks": _masks_for_parity(p), "ident": ident,
        })
        metas.append((b, tiles))

    nc = _get_program(_upto)
    res = run_bass_kernel_spmd(nc, in_maps, core_ids=list(range(8)))

    if _upto != "D":
        return [res.results[c]["out"] for c in range(8)]

    outp = np.empty((B, T, C), np.float32)
    for core in range(8):
        b, tiles = metas[core]
        o = res.results[core]["out"]
        for j in range(OWN):
            g = tiles[j]
            outp[b, g * 128:(g + 1) * 128, :] = o[j * 128:(j + 1) * 128, :]
    return outp
